# version 14
# speedup vs baseline: 1.0847x; 1.0048x over previous
import os
from contextlib import ExitStack

import numpy as np
import ml_dtypes

import concourse.bass as bass
import concourse.mybir as mybir
from concourse.bass_utils import run_bass_kernel_spmd

F32 = mybir.dt.float32
BF16 = mybir.dt.bfloat16
I32 = mybir.dt.int32
AF = mybir.ActivationFunctionType
ALU = mybir.AluOpType
AXX = mybir.AxisListType.X

P = 128
T = 2048
DM = 2048
KC = 16
NTT = 16
H = 16
DK = 128
GH = 2
NG = 8
INTER = 8192
NIC = 64
IG = 8
NE = 8
EPS = 1e-5
ISQ = float(np.float32(1.0) / np.float32(np.sqrt(np.float64(128.0))))
RD = 384.0
RDT = 384.5
I2_24 = 1.0 / 16777216.0
I127 = 1.0 / 127.0
NCORES = 8

STOP = os.environ.get("KMK_STOP", "all")

SEMS = ("w", "q", "g", "g3", "g2", "gc", "t", "t2", "v", "v2", "s",
        "s2", "o", "bar", "pp", "dA", "dB", "dC", "dD", "dE", "dF", "g2b",
        "qb", "ob")


CHAIN_OPS = frozenset((
    "tensor_tensor", "tensor_scalar", "tensor_scalar_mul", "tensor_scalar_add",
    "tensor_scalar_max", "tensor_scalar_min", "tensor_scalar_sub",
    "scalar_tensor_tensor", "tensor_copy", "tensor_reduce", "reduce_max",
    "reduce_sum", "reciprocal", "memset", "memzero", "tensor_tensor_scan",
    "select", "activation", "copy", "sqrt", "square", "add", "mul", "sign",
    "affine_select", "iota", "tensor_add", "tensor_mul", "tensor_sub",
    "tensor_max", "tensor_relu",
))

_AUTO = ("v", "s", "gc", "s2", "v2")


class S:
    def __init__(self, nc, ctx):
        self.h = {}
        self.c = {}
        for name in SEMS:
            self.h[name] = ctx.enter_context(nc.semaphore("sem_" + name))
            self.c[name] = 0
        self.barn = 0
        self.dirty = set()

    def inc(self, name, instr, n=1):
        if name in _AUTO:
            return self.c[name]
        instr.then_inc(self.h[name], n)
        self.c[name] += n
        return self.c[name]

    def val(self, name):
        return self.c[name]


class Stream:

    def __init__(self, sm, names, serial=False):
        self.sm = sm
        self.names = tuple(names)
        self.serial = serial
        self.i = 0

    def issue(self, eng, thunk):
        K = len(self.names)
        nm = self.names[self.i % K]
        if self.serial and self.i >= K:
            eng.wait_ge(self.sm.h[nm], self.sm.c[nm])
        instr = thunk()
        instr.then_inc(self.sm.h[nm], 16)
        self.sm.c[nm] += 16
        self.sm.dirty.add(nm)
        self.i += 1
        return (nm, self.sm.c[nm])

    def future(self, j):
        K = len(self.names)
        idx = self.i + j
        return (self.names[idx % K], 16 * (idx // K + 1))

    def fin(self):
        return [(nm, self.sm.c[nm]) for nm in self.names]


class Chain:

    def __init__(self, eng, sm, name):
        self._eng = eng
        self._sm = sm
        self._name = name

    def __getattr__(self, a):
        f = getattr(self._eng, a)
        if a not in CHAIN_OPS:
            return f
        sm = self._sm
        h = sm.h[self._name]
        nm = self._name

        def g(*args, **kw):
            self._eng.wait_ge(h, sm.c[nm])
            instr = f(*args, **kw)
            instr.then_inc(h, 1)
            sm.c[nm] += 1
            return instr
        return g


class A:
    def __init__(self, nc):
        self.nc = nc
        self.live = []

    def sbuf(self, name, shape, dt):
        cm = self.nc.sbuf_tensor(name, shape, dt)
        h = cm.__enter__()
        self.live.append(cm)
        return h

    def psum(self, name, shape, dt):
        cm = self.nc.psum_tensor(name, shape, dt)
        h = cm.__enter__()
        self.live.append(cm)
        return h

    def free(self):
        for cm in reversed(self.live):
            cm.__exit__(None, None, None)
        self.live = []


def build_program():
    nc = bass.Bass("TRN2", target_bir_lowering=False)

    x_in = nc.dram_tensor("x", [T, DM], F32, kind="ExternalInput").ap()
    wkqv = nc.dram_tensor("L0_attn_Wkqv", [DM, 3 * DM], F32, kind="ExternalInput").ap()
    bkqv = nc.dram_tensor("L0_attn_bkqv", [3 * DM], F32, kind="ExternalInput").ap()
    wo = nc.dram_tensor("L0_attn_Wo", [DM, DM], F32, kind="ExternalInput").ap()
    bo = nc.dram_tensor("L0_attn_bo", [DM], F32, kind="ExternalInput").ap()
    gate_w0 = nc.dram_tensor("L0_ffn_gate_w", [DM, NE], F32, kind="ExternalInput").ap()
    up_w = nc.dram_tensor("L0_ffn_up_w", [DM, INTER], F32, kind="ExternalInput").ap()
    down_w = nc.dram_tensor("L0_ffn_down_w", [INTER, DM], F32, kind="ExternalInput").ap()
    exp_w = nc.dram_tensor("L0_ffn_expert_weight", [DM, DM], F32, kind="ExternalInput").ap()
    ln_g = nc.dram_tensor("L0_ffn_ln_gamma", [DM], F32, kind="ExternalInput").ap()
    ln_b = nc.dram_tensor("L0_ffn_ln_beta", [DM], F32, kind="ExternalInput").ap()
    w1 = nc.dram_tensor("L1_attn_W", [2 * DM, DM], F32, kind="ExternalInput").ap()
    b1 = nc.dram_tensor("L1_attn_b", [2 * DM], F32, kind="ExternalInput").ap()
    gw1 = nc.dram_tensor("L1_ffn_gate_w", [INTER, DM], F32, kind="ExternalInput").ap()
    gb1 = nc.dram_tensor("L1_ffn_gate_b", [INTER], F32, kind="ExternalInput").ap()
    vw1 = nc.dram_tensor("L1_ffn_val_w", [INTER, DM], F32, kind="ExternalInput").ap()
    vb1 = nc.dram_tensor("L1_ffn_val_b", [INTER], F32, kind="ExternalInput").ap()
    ow1 = nc.dram_tensor("L1_ffn_out_w", [DM, INTER], F32, kind="ExternalInput").ap()
    ob1 = nc.dram_tensor("L1_ffn_out_b", [DM], F32, kind="ExternalInput").ap()
    trig2 = nc.dram_tensor("c_trig2", [T, 128], F32, kind="ExternalInput").ap()
    cmask = nc.dram_tensor("c_mask", [P, 4, 512], BF16, kind="ExternalInput").ap()

    out = nc.dram_tensor("out", [T, DM], F32, kind="ExternalOutput").ap()

    y1_acc = nc.dram_tensor("y1_acc", [T, DM], F32, kind="Internal").ap()
    y2_acc = nc.dram_tensor("y2_acc", [T, DM], F32, kind="Internal").ap()
    x0_sp = nc.dram_tensor("x0_sp", [T, DM], BF16, kind="Internal").ap()
    x3_sp = nc.dram_tensor("x3_sp", [T, DM], BF16, kind="Internal").ap()
    x3q_sp = nc.dram_tensor("x3q_sp", [T, DM], BF16, kind="Internal").ap()
    h_sp = nc.dram_tensor("h_sp", [KC, P, T], BF16, kind="Internal").ap()
    gv_sp = nc.dram_tensor("gv_sp", [16, NTT, P, 512], BF16, kind="Internal").ap()
    o_acc = nc.dram_tensor("o_acc", [T, DM], F32, kind="Internal").ap()

    wkqv_r = wkqv.rearrange("(k p) c -> p k c", p=P)
    wo_r = wo.rearrange("(g p) c -> p g c", p=P)
    gate0_r = gate_w0.rearrange("(k p) e -> p k e", p=P)
    up_r = up_w.rearrange("(k p) c -> p k c", p=P)
    down_r = down_w.rearrange("(k p) c -> p k c", p=P)
    exp_r = exp_w.rearrange("(k p) c -> p k c", p=P)
    w1_r = w1.rearrange("(k p) c -> p k c", p=P)
    gw1_r = gw1.rearrange("(k p) c -> p k c", p=P)
    vw1_r = vw1.rearrange("(k p) c -> p k c", p=P)
    ow1_r = ow1.rearrange("(k p) c -> p k c", p=P)
    b1_r = b1.rearrange("(k p) -> p k", p=P)
    x_r = x_in.rearrange("(tt p) c -> p tt c", p=P)
    trig_r = trig2.rearrange("(tt p) c -> p tt c", p=P)
    y1_r = y1_acc.rearrange("(tt p) c -> p tt c", p=P)
    y2_r = y2_acc.rearrange("(tt p) c -> p tt c", p=P)
    x0_r = x0_sp.rearrange("(tt p) c -> p tt c", p=P)
    x3_r = x3_sp.rearrange("(tt p) c -> p tt c", p=P)
    x3q_r = x3q_sp.rearrange("(tt p) c -> p tt c", p=P)
    oacc_r = o_acc.rearrange("(tt p) c -> p tt c", p=P)
    out_r = out.rearrange("(tt p) c -> p tt c", p=P)

    top = ExitStack()
    with top:
        sm = S(nc, top)
        block = top.enter_context(nc.Block())

        def SEC_V(fn):
            block.vector(lambda e: fn(Chain(e, sm, "v")))

        def SEC_S(fn):
            block.scalar(lambda e: fn(Chain(e, sm, "s")))

        def SEC_G(fn):
            block.gpsimd(lambda e: fn(Chain(e, sm, "gc")))
        PA = A(nc)

        ident_b = PA.sbuf("ident_b", [P, P], BF16)
        ident_f = PA.sbuf("ident_f", [P, P], F32)
        ones_r_b = PA.sbuf("ones_r_b", [1, P], BF16)
        ones_r_f = PA.sbuf("ones_r_f", [1, P], F32)
        ones_c_b = PA.sbuf("ones_c_b", [P, 1], BF16)
        onescol = PA.sbuf("onescol", [P, 1], F32)
        svec = PA.sbuf("svec", [P, NTT], F32)
        scale3 = PA.sbuf("scale3", [P, NTT], F32)
        g3r = PA.sbuf("g3r", [P, NTT], F32)
        ggv = PA.sbuf("ggv", [P, NTT], F32)
        scgv = PA.sbuf("scgv", [P, NTT], F32)
        gvr = PA.sbuf("gvr", [P, NTT], F32)
        rv_g = PA.sbuf("rv_g", [P, NTT], F32)
        rv_v = PA.sbuf("rv_v", [P, NTT], F32)
        rv_o = PA.sbuf("rv_o", [P, NTT], F32)
        gsum = PA.sbuf("gsum", [P, 4], F32)
        gamf = PA.sbuf("gamf", [P, 4], F32)
        gamh = PA.sbuf("gamh", [P, 4], F32)
        gamhn = PA.sbuf("gamhn", [P, 4], F32)
        lnscr = PA.sbuf("lnscr", [P, DM], F32)
        st_ns = PA.sbuf("st_ns", [P, 1], F32)
        st_sq = PA.sbuf("st_sq", [P, 1], F32)
        st_nm = PA.sbuf("st_nm", [P, 1], F32)
        st_va = PA.sbuf("st_va", [P, 1], F32)
        st_rs = PA.sbuf("st_rs", [P, 1], F32)
        st_nr = PA.sbuf("st_nr", [P, 1], F32)
        st_t1 = PA.sbuf("st_t1", [P, 1], F32)
        st_t2 = PA.sbuf("st_t2", [P, 1], F32)
        qgA = PA.sbuf("qgA", [P, 1], F32)
        qgB = PA.sbuf("qgB", [P, 1], F32)
        epsc = PA.sbuf("epsc", [P, 1], F32)
        pp_r = PA.sbuf("pp_r", [P, 1], F32)
        ASA = A(nc)
        SA = ASA.sbuf("SA", [P, KC, T], BF16)
        ppst = ASA.sbuf("ppst", [P, DM // 2], F32)

        pp_plan = []
        for ci in range(NIC):
            for hf in range(2):
                pp_plan.append((gw1_r[:, ci, 1024 * hf:1024 * (hf + 1)], 0))
        for ci in range(NIC):
            for hf in range(2):
                pp_plan.append((vw1_r[:, ci, 1024 * hf:1024 * (hf + 1)], 1))
        for ci in range(KC):
            for ms in range(8):
                pp_plan.append((ow1_r[:, ci, 1024 * ms:1024 * (ms + 1)], 2))
        pp_state = {"i": 0, "pending": [], "reduced": 0}

        def emit_prepass(gp, n):
            for _ in range(n):
                i = pp_state["i"]
                if i >= len(pp_plan):
                    return
                view, col = pp_plan[i]
                if i >= 1:
                    gp.wait_ge(sm.h["pp"], i)
                tick = ("g", sm.inc("g", gp.dma_start(out=ppst[:], in_=view),
                                    16))
                pp_state["pending"].append((tick, col, 0))
                pp_state["i"] = i + 1

        def emit_prepass_reduce(keep=1):
            while len(pp_state["pending"]) > keep:
                tick, col, buf = pp_state["pending"].pop(0)

                def _vp(vec, tick=tick, col=col, buf=buf):
                    if pp_state["reduced"] == 0:
                        vec.wait_ge(sm.h["gc"], gc_const)
                    wt(vec, tick)
                    vec.tensor_reduce(pp_r[:], ppst[:], axis=AXX,
                                      op=ALU.add,
                                      apply_absolute_value=True)
                    vec.tensor_tensor(
                        gsum[:, col:col + 1], gsum[:, col:col + 1], pp_r[:],
                        ALU.add)
                    vec.wait_ge(sm.h["v"], sm.val("v"))
                    vec.nop().then_inc(sm.h["pp"], 1)
                    sm.c["pp"] += 1
                SEC_V(_vp)
                pp_state["reduced"] += 1

        def emit_prepass_drain():
            while pp_state["i"] < len(pp_plan) or pp_state["pending"]:
                def _gp(gp):
                    emit_prepass(gp, 1)
                SEC_G(_gp)
                emit_prepass_reduce(keep=0)

        def wt(eng, tick):
            eng.wait_ge(sm.h[tick[0]], tick[1])

        def wt_all(eng, ticks):
            for tk in ticks:
                wt(eng, tk)

        G2 = Stream(sm, ("g2", "g2b"))
        QS = Stream(sm, ("q", "qb"))
        OUT = Stream(sm, ("o", "ob"))
        XQS = Stream(sm, ("dE", "dF"))

        def barrier(extra=()):
            sm.barn += 1
            tgt = 5 * sm.barn
            extra = tuple(extra) + tuple(
                (nm, sm.c[nm]) for nm in sorted(sm.dirty))
            sm.dirty.clear()
            for eng, own in (("sync", None), ("tensor", None),
                             ("vector", "v"), ("scalar", "s"),
                             ("gpsimd", "gc")):
                def _b(e, _t=tgt, _o=own, _first=(eng == "sync"), _x=extra):
                    if _first:
                        for nm, val in _x:
                            e.wait_ge(sm.h[nm], val)
                    if _o is not None:
                        e.wait_ge(sm.h[_o], sm.val(_o))
                    e.nop().then_inc(sm.h["bar"], 1)
                    e.wait_ge(sm.h["bar"], _t)
                getattr(block, eng)(_b)
                sm.c["bar"] += 1

        def emit_ln(src, n_feat, pre_wait=()):
            inv = 1.0 / float(n_feat)

            def _v1(vec):
                for nm, val in pre_wait:
                    vec.wait_ge(sm.h[nm], val)
                vec.tensor_reduce(st_ns[:], src, axis=AXX, op=ALU.add, negate=True)
                sm.inc("v", vec.tensor_scalar(st_nm[:], st_ns[:], inv, None, ALU.mult))
            SEC_V(_v1)

            def _s1(sca):
                for nm, val in pre_wait:
                    sca.wait_ge(sm.h[nm], val)
                sm.inc("s", sca.activation(lnscr[:, 0:n_feat], src, AF.Square,
                                           accum_out=st_sq[:]))
            SEC_S(_s1)
            s_sq = sm.val("s")

            def _v2(vec):
                vec.wait_ge(sm.h["s"], s_sq)
                vec.tensor_tensor(st_t1[:], st_nm[:], st_nm[:], ALU.mult)
                sm.inc("v", vec.scalar_tensor_tensor(
                    out=st_va[:], in0=st_sq[:], scalar=inv, in1=st_t1[:],
                    op0=ALU.mult, op1=ALU.subtract))
            SEC_V(_v2)
            v_var = sm.val("v")

            def _s2(sca):
                sca.wait_ge(sm.h["v"], v_var)
                sm.inc("s", sca.activation(st_t2[:], st_va[:], AF.Sqrt, bias=epsc[:]))
            SEC_S(_s2)
            s_std = sm.val("s")

            def _v3(vec):
                vec.wait_ge(sm.h["s"], s_std)
                vec.reciprocal(st_rs[:], st_t2[:])
                sm.inc("v", vec.tensor_tensor(st_nr[:], st_nm[:], st_rs[:], ALU.mult))
            SEC_V(_v3)
            return sm.val("v"), s_std

        A1a = A(nc)
        ctb = A1a.sbuf("ctb", [P, NTT, 128], BF16)
        Act = A(nc)
        ctile = Act.sbuf("ctile", [P, NTT, 128], F32)
        A0 = A(nc)
        xst = [A0.sbuf(f"xst{i}", [P, DM], F32) for i in range(2)]
        xbf = [A0.sbuf(f"xbf{i}", [P, DM], BF16) for i in range(2)]
        pst0 = [A0.psum(f"pst0{i}", [P, 1024], BF16) for i in range(2)]

        def _g0(gp):
            sm.inc("gc", gp.memset(ident_b[:], 0.0))
            gp.wait_ge(sm.h["gc"], sm.val("gc"))
            sm.inc("gc", gp.affine_select(
                out=ident_b[:], in_=ident_b[:],
                compare_op=ALU.not_equal, fill=1.0, base=0,
                pattern=[[-1, P]], channel_multiplier=1))
            gp.wait_ge(sm.h["gc"], sm.val("gc"))
            sm.inc("gc", gp.memset(ident_f[:], 0.0))
            gp.wait_ge(sm.h["gc"], sm.val("gc"))
            sm.inc("gc", gp.affine_select(
                out=ident_f[:], in_=ident_f[:],
                compare_op=ALU.not_equal, fill=1.0, base=0,
                pattern=[[-1, P]], channel_multiplier=1))
            gp.wait_ge(sm.h["gc"], sm.val("gc"))
            gp.memset(ones_r_b[:], 1.0)
            gp.memset(ones_r_f[:], 1.0)
            gp.memset(ones_c_b[:], 1.0)
            gp.memset(onescol[:], 1.0)
            gp.memset(gsum[:], 0.0)
            gp.memset(epsc[:], EPS)
            sm.inc("gc", gp.memset(ggv[:], 0.0))
            gp.wait_ge(sm.h["gc"], sm.val("gc"))
        SEC_G(_g0)
        gc_const = sm.val("gc")

        TRG = Stream(sm, ("dA",))
        XLD = Stream(sm, ("dB", "dC"))
        XSP = Stream(sm, ("dD", "dE"))
        ld_w = []
        sp_w = []
        w_trig = [None]

        def _sy0(sy):
            w_trig[0] = TRG.issue(sy, lambda: sy.dma_start(out=ctile[:],
                                                           in_=trig_r[:]))
            for tt in range(NTT):
                if tt >= 2:
                    sy.wait_ge(sm.h["v"], tt - 1)
                ld_w.append(XLD.issue(sy, lambda tt=tt: sy.dma_start(
                    out=xst[tt % 2][:], in_=x_r[:, tt, :])))
                if tt >= 1:
                    sy.wait_ge(sm.h["v"], tt)
                    sp_w.append(XSP.issue(sy, lambda tt=tt: sy.dma_start(
                        out=x0_r[:, tt - 1, :], in_=xbf[(tt - 1) % 2][:])))
            sy.wait_ge(sm.h["v"], NTT)
            sp_w.append(XSP.issue(sy, lambda: sy.dma_start(
                out=x0_r[:, NTT - 1, :], in_=xbf[(NTT - 1) % 2][:])))
        block.sync(_sy0)

        def _v0(vec):
            for tt in range(NTT):
                wt(vec, ld_w[tt])
                if tt >= 2:
                    vec.wait_ge(sm.h["t"], 2 * (tt - 1))
                    wt(vec, sp_w[tt - 2])
                sm.inc("v", vec.tensor_copy(xbf[tt % 2][:], xst[tt % 2][:]))
        SEC_V(_v0)

        def _t0(te, ):
            te.wait_ge(sm.h["gc"], gc_const)
            for tt in range(NTT):
                te.wait_ge(sm.h["v"], tt + 1)
                for half in range(2):
                    nb = 2 * tt + half
                    if nb >= 2:
                        te.wait_ge(sm.h["s"], nb - 1)
                    mm = None
                    for j in range(8):
                        k = 8 * half + j
                        mm = te.transpose(pst0[half][:, 128 * j:128 * (j + 1)],
                                          xbf[tt % 2][:, 128 * k:128 * (k + 1)],
                                          ident_b[:])
                    sm.inc("t", mm)
        block.tensor(_t0)

        def _s0(sca):
            for tt in range(NTT):
                for half in range(2):
                    sca.wait_ge(sm.h["t"], 2 * tt + half + 1)
                    dst = SA[:, 8 * half:8 * half + 8, 128 * tt:128 * (tt + 1)]
                    sm.inc("s", sca.copy(dst, pst0[half][:]))
        SEC_S(_s0)

        def _vtc(vec):
            wt(vec, w_trig[0])
            sm.inc("v", vec.tensor_copy(ctb[:], ctile[:]))
        SEC_V(_vtc)

        for _pp in range(20):
            def _gpp(gp):
                emit_prepass(gp, 1)
            SEC_G(_gpp)
            emit_prepass_reduce()

        barrier()
        A0.free()
        Act.free()

        A1 = A(nc)
        masks = A1.sbuf("masks", [P, 4, 512], BF16)
        wst = [A1.sbuf(f"wst{i}", [P, 4, GH * DK], F32) for i in range(2)]
        wqb = A1.sbuf("wqb", [P, KC, GH * DK], BF16)
        wkb = A1.sbuf("wkb", [P, KC, GH * DK], BF16)
        wvb = A1.sbuf("wvb", [P, KC, GH * DK], BF16)
        qtg = A1.sbuf("qtg", [P, NTT, GH * DK], BF16)
        ktg = A1.sbuf("ktg", [P, NTT, GH * DK], BF16)
        vtg = A1.sbuf("vtg", [P, NTT, GH, DK], BF16)
        rt1 = A1.sbuf("rt1", [P, T], BF16)
        rt2 = A1.sbuf("rt2", [P, T], BF16)
        rt3 = A1.sbuf("rt3", [P, 1024], BF16)
        qT = A1.sbuf("qT", [P, GH, T], BF16)
        kT = A1.sbuf("kT", [P, GH, T], BF16)
        OT = A1.sbuf("OT", [P, GH, T], BF16)
        PTb = [A1.sbuf(f"PTb{i}", [P, 512], BF16) for i in range(2)]
        drow = A1.sbuf("drow", [1, 512], F32)
        rrow = A1.sbuf("rrow", [1, 512], F32)
        rcb = A1.sbuf("rcb", [P, 512], F32)
        bq_r = A1.sbuf("bq_r", [1, GH * DK], BF16)
        bk_r = A1.sbuf("bk_r", [1, GH * DK], BF16)
        bv_r = A1.sbuf("bv_r", [1, GH * DK], BF16)
        bob_t = A1.sbuf("bob_t", [1, DM], BF16)
        ytl = [A1.sbuf(f"ytl{i}", [P, DM], F32) for i in range(2)]
        x0tk = [rt1, rt2]

        MS1 = Stream(sm, ("dA",), serial=True)
        WST = Stream(sm, ("dB", "dC"))
        WOF = Stream(sm, ("dF",))
        XTK = Stream(sm, ("dD", "dE"))
        w_msk = [None]

        def _sy1(sy):
            w_msk[0] = MS1.issue(sy, lambda: sy.dma_start(out=masks[:],
                                                          in_=cmask[:]))
        block.sync(_sy1)

        w_bi = [None]

        def _syb(sy):
            w_bi[0] = MS1.issue(sy, lambda: sy.dma_start(out=lnscr[0:1, :],
                                                         in_=bo[None, :]))
        block.sync(_syb)

        def _vbc(vec):
            wt(vec, w_bi[0])
            sm.inc("v", vec.tensor_copy(bob_t[:], lnscr[0:1, :]))
        SEC_V(_vbc)

        cv2 = ctb[:].rearrange("p tt (q i) -> p tt q i", q=2)
        COS = cv2[:, :, 0, :]
        SIN = cv2[:, :, 1, :]

        wofl = wst[0][:].rearrange("p k c -> p (k c)")
        wogb = wqb[:].rearrange("p k c -> p (k c)")
        wogv = wogb.rearrange("p (hl c) -> p hl c", hl=GH)

        for g in range(NG):
            c0 = g * GH * DK
            ABg = A(nc)
            psq = [ABg.psum(f"psq{i}_{g}", [P, 512], F32) for i in range(2)]
            psk = [ABg.psum(f"psk{i}_{g}", [P, 512], F32) for i in range(2)]
            psv = [ABg.psum(f"psv{i}_{g}", [P, 512], F32) for i in range(2)]
            pstp = [ABg.psum(f"pstp{i}_{g}", [P, 1024], BF16) for i in range(2)]

            bav = sm.val("v")
            wo_free = sm.val("t")
            wst_w = []

            def _sa(sy, c0=c0, bav=bav, g=g):
                for wi, coff in enumerate((DM, 0, 2 * DM)):
                    for qq in range(4):
                        ch = 4 * wi + qq
                        if g == 0 and ch < 2:
                            pass
                        elif g == 0:
                            sy.wait_ge(sm.h["v"], bav + ch - 1)
                        else:
                            sy.wait_ge(sm.h["v"], bav + max(ch - 1, 0))
                        wst_w.append(WST.issue(sy, lambda c0=c0, qq=qq, ch=ch,
                                               coff=coff: sy.dma_start(
                            out=wst[ch % 2][:],
                            in_=wkqv_r[:, 4 * qq:4 * qq + 4,
                                       coff + c0:coff + c0 + GH * DK])))
            block.sync(_sa)

            def _va(vec, wo_free=wo_free):
                for wi, dst in enumerate((wqb, wkb, wvb)):
                    for qq in range(4):
                        ch = 4 * wi + qq
                        if ch == 0:
                            vec.wait_ge(sm.h["t"], wo_free)
                        wt(vec, wst_w[ch])
                        sm.inc("v", vec.tensor_copy(dst[:, 4 * qq:4 * qq + 4, :],
                                                    wst[ch % 2][:]))
            SEC_V(_va)

            for wi, (coff, dstb) in enumerate(((DM, bq_r), (0, bk_r),
                                              (2 * DM, bv_r))):
                w_b = [None]

                def _sab(sy, c0=c0, coff=coff, wi=wi, g=g, w_b=w_b):
                    sy.wait_ge(sm.h["v"], sm.val("v"))
                    if wi == 0:
                        sy.wait_ge(sm.h["t"], sm.val("t"))
                    w_b[0] = MS1.issue(sy, lambda: sy.dma_start(
                        out=lnscr[0:1, 0:GH * DK],
                        in_=bkqv[None, coff + c0:coff + c0 + GH * DK]))
                block.sync(_sab)

                def _vab(vec, dstb=dstb, w_b=w_b):
                    wt(vec, w_b[0])
                    sm.inc("v", vec.tensor_copy(dstb[:], lnscr[0:1, 0:GH * DK]))
                SEC_V(_vab)

            bbt = sm.val("t")
            bbs = sm.val("s")
            bbv = sm.val("v")

            def _tb(te, c0=c0, bbs=bbs, bbv=bbv):
                te.wait_ge(sm.h["v"], bbv)
                te.wait_ge(sm.h["s"], bbs)
                for tt in range(NTT):
                    if tt >= 2:
                        te.wait_ge(sm.h["s"], bbs + 3 * tt - 3)
                    for kd in range(KC):
                        st = (kd == 0)
                        sl = SA[:, kd, 128 * tt:128 * (tt + 1)]
                        te.matmul(psq[tt % 2][:, 0:GH * DK], sl, wqb[:, kd, :],
                                  start=st, stop=False)
                        te.matmul(psk[tt % 2][:, 0:GH * DK], sl, wkb[:, kd, :],
                                  start=st, stop=False)
                        te.matmul(psv[tt % 2][:, 0:GH * DK], sl, wvb[:, kd, :],
                                  start=st, stop=False)
                    te.matmul(psq[tt % 2][:, 0:GH * DK], ones_r_b[:], bq_r[:],
                              start=False, stop=True)
                    te.matmul(psk[tt % 2][:, 0:GH * DK], ones_r_b[:], bk_r[:],
                              start=False, stop=True)
                    sm.inc("t", te.matmul(psv[tt % 2][:, 0:GH * DK], ones_r_b[:],
                                          bv_r[:], start=False, stop=True))
            block.tensor(_tb)

            def _sb(sca, bbt=bbt):
                for tt in range(NTT):
                    sca.wait_ge(sm.h["t"], bbt + tt + 1)
                    sca.copy(qtg[:, tt, :], psq[tt % 2][:, 0:GH * DK])
                    sca.copy(ktg[:, tt, :], psk[tt % 2][:, 0:GH * DK])
                    sm.inc("s", sca.copy(vtg[:, tt, :, :],
                                         psv[tt % 2][:, 0:GH * DK]))
            SEC_S(_sb)

            ev_done = sm.val("s")

            def _vb(vec, ev_done=ev_done):
                vec.wait_ge(sm.h["s"], ev_done)
                HT = NTT * 64
                for tok in (qtg, ktg):
                    v5 = tok[:].rearrange("p tt (hl i two) -> p tt hl i two",
                                          hl=GH, two=2)
                    for hl in range(GH):
                        Ev = v5[:, :, hl, :, 0]
                        Ov = v5[:, :, hl, :, 1]
                        vec.tensor_tensor(rt1[:, 0:HT], Ev, COS, ALU.mult)
                        vec.tensor_tensor(rt2[:, 0:HT], Ev, SIN, ALU.mult)
                        vec.tensor_tensor(rt3[:], Ov, SIN, ALU.mult)
                        vec.tensor_tensor(Ev, rt1[:, 0:HT], rt3[:],
                                          ALU.subtract)
                        vec.tensor_tensor(rt3[:], Ov, COS, ALU.mult)
                        sm.inc("v", vec.tensor_tensor(Ov, rt2[:, 0:HT],
                                                      rt3[:], ALU.add))
            SEC_V(_vb)

            rope_done = sm.val("v")
            btp = sm.val("t")
            bsp = sm.val("s")

            def _tb2(te, rope_done=rope_done, bsp=bsp):
                te.wait_ge(sm.h["v"], rope_done)
                nb = 0
                for src in (qtg, ktg):
                    for tt in range(NTT):
                        for hl in range(GH):
                            buf = nb // 8
                            j = nb % 8
                            if j == 0 and buf >= 2:
                                te.wait_ge(sm.h["s"], bsp + buf - 1)
                            mm = te.transpose(
                                pstp[buf % 2][:, 128 * j:128 * (j + 1)],
                                src[:, tt, 128 * hl:128 * (hl + 1)], ident_b[:])
                            if j == 7:
                                sm.inc("t", mm)
                            nb += 1
            block.tensor(_tb2)

            def _sb2(sca, btp=btp):
                for b in range(8):
                    sca.wait_ge(sm.h["t"], btp + b + 1)
                    dstT = qT if b < 4 else kT
                    tt0 = (b % 4) * 4
                    dv = dstT[:, :, 128 * tt0:128 * (tt0 + 4)].rearrange(
                        "p hl (a c) -> p a hl c", c=128)
                    sm.inc("s", sca.copy(dv, pstp[b % 2][:]))
            SEC_S(_sb2)

            qk_done = sm.val("s")
            ABg.free()
            ACg = A(nc)
            psS = [ACg.psum(f"psS{i}_{g}", [P, 512], F32) for i in range(2)]
            psPV = ACg.psum(f"psPV_{g}", [P, 512], F32)
            psDen = ACg.psum(f"psDen_{g}", [P, 512], F32)
            psRB = ACg.psum(f"psRB_{g}", [P, 512], F32)

            for hl in range(GH):
                for qs in range(4):
                    n_ch = 4 * qs + 4
                    bt2 = sm.val("t2")
                    bs2 = sm.val("s")
                    bv2 = sm.val("v")
                    btP = sm.val("t")
                    bsA = sm.val("s")
                    bvA = sm.val("v")

                    def _tc(te, hl=hl, qs=qs, n_ch=n_ch, bs2=bs2, bv2=bv2,
                            qk_done=qk_done, bsA=bsA, bvA=bvA):
                        te.wait_ge(sm.h["s"], max(qk_done, bsA))
                        te.wait_ge(sm.h["v"], bvA)
                        for i in range(n_ch):
                            if i >= 2:
                                te.wait_ge(sm.h["s"], bs2 + i - 1)
                            sm.inc("t2", te.matmul(
                                psS[i % 2][:],
                                kT[:, hl, 128 * i:128 * (i + 1)],
                                qT[:, hl, 512 * qs:512 * (qs + 1)],
                                start=True, stop=True))
                            if i >= 1:
                                j = i - 1
                                if j >= 4 * qs:
                                    te.wait_ge(sm.h["v"], bv2 + (j - 4 * qs) + 1)
                                else:
                                    te.wait_ge(sm.h["s"], bs2 + j + 1)
                                te.matmul(psPV[:], vtg[:, j, hl, :], PTb[j % 2][:],
                                          start=(j == 0), stop=False)
                                sm.inc("t", te.matmul(
                                    psDen[0:1, :], ones_c_b[:], PTb[j % 2][:],
                                    start=(j == 0), stop=False))
                        j = n_ch - 1
                        te.wait_ge(sm.h["v"], bv2 + 4)
                        te.matmul(psPV[:], vtg[:, j, hl, :], PTb[j % 2][:],
                                  start=(j == 0), stop=True)
                        sm.inc("t", te.matmul(
                            psDen[0:1, :], ones_c_b[:], PTb[j % 2][:],
                            start=(j == 0), stop=True))
                    block.tensor(_tc)

                    def _sc(sca, n_ch=n_ch, bt2=bt2, btP=btP):
                        for i in range(n_ch):
                            sca.wait_ge(sm.h["t2"], bt2 + i + 1)
                            if i >= 2:
                                sca.wait_ge(sm.h["t"], btP + i - 1)
                            else:
                                sca.wait_ge(sm.h["t"], btP)
                            sm.inc("s2", sca.activation(
                                PTb[i % 2][:], psS[i % 2][:], AF.Exp, scale=ISQ))
                    SEC_S(_sc)

                    def _vc(vec, qs=qs, bs2=bs2, g=g, hl=hl):
                        if g == 0 and hl == 0 and qs == 0:
                            wt(vec, w_msk[0])
                        for jj in range(4):
                            j = 4 * qs + jj
                            vec.wait_ge(sm.h["s"], bs2 + j + 1)
                            sm.inc("v2", vec.tensor_tensor(
                                PTb[j % 2][:], PTb[j % 2][:], masks[:, jj, :],
                                ALU.mult))
                    SEC_V(_vc)

                    pv_done = sm.val("t")

                    def _sn(sca, pv_done=pv_done):
                        sca.wait_ge(sm.h["t"], pv_done)
                        sm.inc("s", sca.copy(drow[:], psDen[0:1, :]))
                    SEC_S(_sn)
                    s_den = sm.val("s")

                    def _vn(vec, s_den=s_den):
                        vec.wait_ge(sm.h["s"], s_den)
                        sm.inc("v", vec.reciprocal(rrow[:], drow[:]))
                    SEC_V(_vn)
                    v_rec = sm.val("v")

                    def _tn(te, v_rec=v_rec):
                        te.wait_ge(sm.h["v"], v_rec)
                        sm.inc("t", te.matmul(psRB[:], ones_r_f[:], rrow[:],
                                              start=True, stop=True))
                    block.tensor(_tn)
                    t_rb = sm.val("t")

                    def _sn2(sca, t_rb=t_rb):
                        sca.wait_ge(sm.h["t"], t_rb)
                        sm.inc("s", sca.copy(rcb[:], psRB[:]))
                    SEC_S(_sn2)
                    s_rcb = sm.val("s")

                    def _vn2(vec, hl=hl, qs=qs, s_rcb=s_rcb):
                        vec.wait_ge(sm.h["s"], s_rcb)
                        sm.inc("v", vec.tensor_tensor(
                            OT[:, hl, 512 * qs:512 * (qs + 1)], psPV[:], rcb[:],
                            ALU.mult))
                    SEC_V(_vn2)

                    def _gpp(gp):
                        emit_prepass(gp, 1)
                    SEC_G(_gpp)
                    emit_prepass_reduce()

            ot_done = sm.val("v")
            ACg.free()
            ADg = A(nc)
            psO = [ADg.psum(f"psO{i}_{g}", [P, 512], F32) for i in range(2)]

            bds = sm.val("s")
            bdg2 = G2.fin()
            wof_w = []

            bqkv_free = sm.val("t")
            bdv0 = sm.val("v")

            def _sd(sy, g=g, bdv0=bdv0):
                for qq in range(4):
                    if qq >= 1:
                        sy.wait_ge(sm.h["v"], bdv0 + qq)
                    else:
                        sy.wait_ge(sm.h["v"], bdv0)
                    wof_w.append(WOF.issue(sy, lambda g=g, qq=qq: sy.dma_start(
                        out=wofl,
                        in_=wo_r[:, GH * g + qq // 2,
                                 1024 * (qq % 2):1024 * (qq % 2) + 1024])))
            block.sync(_sd)

            def _vd(vec, bqkv_free=bqkv_free):
                for qq in range(4):
                    wt(vec, wof_w[qq])
                    if qq == 0:
                        vec.wait_ge(sm.h["t"], bqkv_free)
                    sm.inc("v", vec.tensor_copy(
                        wogb[:, 1024 * qq:1024 * (qq + 1)], wofl))
            SEC_V(_vd)
            v_wo = sm.val("v")

            td_ticks = []
            g2_ticks = []
            w_x0l = []
            for tt in range(NTT):
                if g == 0:
                    def _sdx(sy, tt=tt, v_wo=v_wo):
                        if tt < 2:
                            sy.wait_ge(sm.h["v"], v_wo)
                        if tt >= 2:
                            sy.wait_ge(sm.h["t"], td_ticks[4 * (tt - 1) - 1])
                        wt(sy, sp_w[tt])
                        w_x0l.append(XTK.issue(sy, lambda tt=tt: sy.dma_start(
                            out=x0tk[tt % 2][:], in_=x0_r[:, tt, :])))
                    block.sync(_sdx)

                def _td(te, g=g, tt=tt, v_wo=v_wo, bds=bds):
                    if tt == 0:
                        te.wait_ge(sm.h["v"], v_wo)
                    if g == 0:
                        wt(te, w_x0l[tt])
                    for dc in range(4):
                        nb = 4 * tt + dc
                        if nb >= 2:
                            te.wait_ge(sm.h["s"], bds + nb - 1)
                        for hl in range(GH):
                            last = (g != 0 and hl == GH - 1)
                            mm = te.matmul(psO[nb % 2][:],
                                           OT[:, hl, 128 * tt:128 * (tt + 1)],
                                           wogv[:, hl, 512 * dc:512 * (dc + 1)],
                                           start=(hl == 0), stop=last)
                        if g == 0:
                            te.matmul(psO[nb % 2][:], ident_b[:],
                                      x0tk[tt % 2][:, 512 * dc:512 * (dc + 1)],
                                      start=False, stop=False)
                            mm = te.matmul(psO[nb % 2][:], ones_r_b[:],
                                           bob_t[0:1, 512 * dc:512 * (dc + 1)],
                                           start=False, stop=True)
                        sm.inc("t", mm)
                        td_ticks.append(sm.val("t"))
                block.tensor(_td)

                def _sd2(sca, tt=tt, g=g, bdg2=bdg2):
                    if tt >= 2:
                        wt(sca, g2_ticks[tt - 2])
                    else:
                        wt_all(sca, bdg2)
                    for dc in range(4):
                        sca.wait_ge(sm.h["t"], td_ticks[4 * tt + dc])
                        sm.inc("s", sca.copy(
                            ytl[tt % 2][:, 512 * dc:512 * (dc + 1)],
                            psO[(4 * tt + dc) % 2][:]))
                SEC_S(_sd2)
                s_ev = sm.val("s")

                def _gd(gp, g=g, tt=tt, s_ev=s_ev, bdg2=bdg2):
                    if tt == 0:
                        wt_all(gp, bdg2)
                    gp.wait_ge(sm.h["s"], s_ev)
                    if g == 0:
                        g2_ticks.append(G2.issue(gp, lambda tt=tt: gp.dma_start(
                            out=y1_r[:, tt, :], in_=ytl[tt % 2][:])))
                    else:
                        g2_ticks.append(G2.issue(gp, lambda tt=tt: gp.dma_start(
                            out=y1_r[:, tt, :], in_=ytl[tt % 2][:],
                            accum_op=ALU.add)))
                    emit_prepass(gp, 1)
                SEC_G(_gd)
                emit_prepass_reduce()
            ADg.free()

        barrier()

        A2p = A(nc)
        pstp = [A2p.psum(f"pstp2{i}", [P, 1024], BF16) for i in range(2)]
        y1ld = ytl
        Y1L = Stream(sm, ("dA", "dB"))
        t_tp_ticks = []
        s_ev_ticks = []
        prev_apply = sm.val("s")
        w_ticks2 = []

        for tt in range(NTT):
            def _sy2(sy, tt=tt, g2y=g2_ticks[NTT - 2 + (tt % 2)]):
                wt(sy, g2y)
                if tt >= 2:
                    sy.wait_ge(sm.h["t"], t_tp_ticks[2 * (tt - 2) + 1])
                w_ticks2.append(Y1L.issue(sy, lambda tt=tt: sy.dma_start(
                    out=y1ld[tt % 2][:], in_=y1_r[:, tt, :])))
            block.sync(_sy2)

            v_ln, s_ln = emit_ln(y1ld[tt % 2][:], DM,
                                 pre_wait=(w_ticks2[tt],))

            def _vap(vec, tt=tt):
                if tt >= 2:
                    vec.wait_ge(sm.h["t"], t_tp_ticks[2 * (tt - 2) + 1])
                dst = rt1 if tt % 2 == 0 else rt2
                sm.inc("v", vec.tensor_scalar(dst[:], y1ld[tt % 2][:],
                                              st_nm[:], st_rs[:],
                                              ALU.add, ALU.mult))
            SEC_V(_vap)
            v_ap = sm.val("v")
            prev_apply = sm.val("s")

            def _ttp(te, tt=tt, v_ap=v_ap):
                te.wait_ge(sm.h["v"], v_ap)
                src = rt1 if tt % 2 == 0 else rt2
                for half in range(2):
                    nb = 2 * tt + half
                    if nb >= 2:
                        wt(te, s_ev_ticks[nb - 2])
                    mm = None
                    for j in range(8):
                        k = 8 * half + j
                        mm = te.transpose(pstp[half][:, 128 * j:128 * (j + 1)],
                                          src[:, 128 * k:128 * (k + 1)], ident_b[:])
                    sm.inc("t", mm)
                    t_tp_ticks.append(sm.val("t"))
            block.tensor(_ttp)

            def _sev(sca, tt=tt):
                for half in range(2):
                    sca.wait_ge(sm.h["t"], t_tp_ticks[2 * tt + half])
                    dst = SA[:, 8 * half:8 * half + 8, 128 * tt:128 * (tt + 1)]
                    sca.copy(dst, pstp[half][:])
                    s_ev_ticks.append(("s", sm.val("s")))
            SEC_S(_sev)

        barrier()

        def emit_dump_stream(dps, dtk, dft):
            dt_t = []
            dt_s = []
            dt_v = []
            dt_o = []
            for tt in range(NTT):
                def _t(te, tt=tt):
                    for half in range(2):
                        nb = 2 * tt + half
                        if nb >= 2:
                            te.wait_ge(sm.h["s"], dt_s[nb - 2])
                        mm = None
                        for j in range(8):
                            k = 8 * half + j
                            mm = te.transpose(dps[half][:, 128 * j:128 * (j + 1)],
                                              SA[:, k, 128 * tt:128 * (tt + 1)],
                                              ident_b[:])
                        sm.inc("t", mm)
                        dt_t.append(sm.val("t"))
                block.tensor(_t)

                def _s(sca, tt=tt):
                    if tt >= 2:
                        sca.wait_ge(sm.h["v"], dt_v[tt - 2])
                    for half in range(2):
                        sca.wait_ge(sm.h["t"], dt_t[2 * tt + half])
                        dv = dtk[tt % 2][:].rearrange("p (a c) -> p a c", c=128)
                        sm.inc("s", sca.copy(dv[:, 8 * half:8 * half + 8, :],
                                             dps[half][:]))
                        dt_s.append(sm.val("s"))
                SEC_S(_s)

                def _v(vec, tt=tt):
                    vec.wait_ge(sm.h["s"], dt_s[2 * tt + 1])
                    if tt >= 2:
                        wt(vec, dt_o[tt - 2])
                    sm.inc("v", vec.tensor_copy(dft[tt % 2][:], dtk[tt % 2][:]))
                    dt_v.append(sm.val("v"))
                SEC_V(_v)

                def _sy(sy, tt=tt):
                    sy.wait_ge(sm.h["v"], dt_v[tt])
                    dt_o.append(OUT.issue(sy, lambda tt=tt: sy.dma_start(
                        out=out_r[:, tt, :], in_=dft[tt % 2][:])))
                block.sync(_sy)

            def _syf(sy):
                wt_all(sy, OUT.fin())
            block.sync(_syf)
            barrier()

        if STOP == "x1":
            emit_dump_stream(pstp, [rt1, rt2], ytl)
            A2p.free()
            A1.free()
            A1a.free()
            return nc

        A2p.free()
        A1.free()
        A1a.free()

        A3a = A(nc)
        g0st = A3a.sbuf("g0st", [P, KC, NE], F32)
        g0b = A3a.sbuf("g0b", [P, KC, NE], BF16)
        lgtok = A3a.sbuf("lgtok", [P, NTT, NE], F32)
        tp2a = A3a.sbuf("tp2a", [P, NE], F32)
        tp2b = A3a.sbuf("tp2b", [P, 1], F32)
        tp2c = A3a.sbuf("tp2c", [P, 1], F32)
        tp2d = A3a.sbuf("tp2d", [P, 1], F32)
        psL = [A3a.psum(f"psL{i}", [P, 512], F32) for i in range(2)]

        w_g0 = [None]

        def _sy3(sy):
            with nc.allow_non_contiguous_dma(reason="tiny gate weight"):
                w_g0[0] = Stream(sm, ("dA",)).issue(sy, lambda: sy.dma_start(
                    out=g0st[:], in_=gate0_r[:]))
        block.sync(_sy3)

        def _v3g(vec):
            wt(vec, w_g0[0])
            sm.inc("v", vec.tensor_copy(g0b[:], g0st[:]))
        SEC_V(_v3g)
        v_g0 = sm.val("v")

        lg_t = []
        lg_s = []
        for tt in range(NTT):
            def _t3(te, tt=tt, v_g0=v_g0):
                if tt == 0:
                    te.wait_ge(sm.h["v"], v_g0)
                if tt >= 2:
                    te.wait_ge(sm.h["s"], lg_s[tt - 2])
                for kd in range(KC):
                    mm = te.matmul(psL[tt % 2][:, 0:NE],
                                   SA[:, kd, 128 * tt:128 * (tt + 1)],
                                   g0b[:, kd, :],
                                   start=(kd == 0), stop=(kd == KC - 1))
                sm.inc("t", mm)
                lg_t.append(sm.val("t"))
            block.tensor(_t3)

            def _s3(sca, tt=tt):
                sca.wait_ge(sm.h["t"], lg_t[tt])
                sm.inc("s", sca.copy(lgtok[:, tt, :], psL[tt % 2][:, 0:NE]))
                lg_s.append(sm.val("s"))
            SEC_S(_s3)

        def _v3b(vec):
            vec.wait_ge(sm.h["s"], lg_s[-1])
            last = None
            for tt in range(NTT):
                lg = lgtok[:, tt, :]
                vec.tensor_reduce(tp2b[:], lg, axis=AXX, op=ALU.max)
                vec.tensor_scalar(tp2a[:], lg, tp2b[:], None, ALU.is_ge)
                vec.tensor_reduce(tp2c[:], tp2a[:], axis=AXX, op=ALU.add)
                vec.scalar_tensor_tensor(out=tp2a[:], in0=tp2a[:], scalar=-1e30,
                                         in1=lg, op0=ALU.mult, op1=ALU.add)
                vec.tensor_reduce(tp2d[:], tp2a[:], axis=AXX, op=ALU.max)
                vec.tensor_scalar(tp2c[:], tp2c[:], 1.5, None, ALU.is_gt)
                vec.tensor_tensor(st_t1[:], tp2b[:], tp2d[:], ALU.subtract)
                vec.tensor_tensor(st_t1[:], st_t1[:], tp2c[:], ALU.mult)
                vec.tensor_tensor(tp2d[:], tp2d[:], st_t1[:], ALU.add)
                last = vec.tensor_tensor(svec[:, tt:tt + 1], tp2b[:], tp2d[:],
                                         ALU.add)
            sm.inc("v", last)
        SEC_V(_v3b)
        v_svec = sm.val("v")

        barrier()
        A3a.free()
        A3b = A(nc)
        expst = A3b.sbuf("expst", [P, KC, 512], F32)
        expb = A3b.sbuf("expb", [P, KC, 512], BF16)
        mtl = [A3b.sbuf(f"mtl{i}", [P, 512], F32) for i in range(2)]
        psM = [A3b.psum(f"psM{i}", [P, 512], F32) for i in range(2)]

        m_w = []
        m_v = []
        m_t = []
        m_s = []
        m_g = []
        EXL = Stream(sm, ("dB",))
        for dc in range(4):
            def _sym(sy, dc=dc):
                if dc >= 1:
                    sy.wait_ge(sm.h["v"], m_v[dc - 1])
                m_w.append(EXL.issue(sy, lambda dc=dc: sy.dma_start(
                    out=expst[:], in_=exp_r[:, :, 512 * dc:512 * (dc + 1)])))
            block.sync(_sym)

            def _vm(vec, dc=dc):
                wt(vec, m_w[dc])
                if dc >= 1:
                    vec.wait_ge(sm.h["t"], m_t[NTT * dc - 1])
                sm.inc("v", vec.tensor_copy(expb[:], expst[:]))
                m_v.append(sm.val("v"))
            SEC_V(_vm)

            for tt in range(NTT):
                nb = NTT * dc + tt

                def _tm(te, dc=dc, tt=tt, nb=nb, v_svec=v_svec):
                    if nb == 0:
                        te.wait_ge(sm.h["v"], v_svec)
                    if tt == 0:
                        te.wait_ge(sm.h["v"], m_v[dc])
                    if nb >= 2:
                        te.wait_ge(sm.h["s"], m_s[nb - 2])
                    for kd in range(KC):
                        mm = te.matmul(psM[nb % 2][:],
                                       SA[:, kd, 128 * tt:128 * (tt + 1)],
                                       expb[:, kd, :],
                                       start=(kd == 0), stop=(kd == KC - 1))
                    sm.inc("t", mm)
                    m_t.append(sm.val("t"))
                block.tensor(_tm)

                def _sm3(sca, tt=tt, nb=nb):
                    sca.wait_ge(sm.h["t"], m_t[nb])
                    if nb >= 2:
                        wt(sca, m_g[nb - 2])
                    sm.inc("s", sca.activation(mtl[nb % 2][:], psM[nb % 2][:],
                                               AF.Copy, scale=svec[:, tt:tt + 1]))
                    m_s.append(sm.val("s"))
                SEC_S(_sm3)

                def _gm(gp, dc=dc, tt=tt, nb=nb):
                    gp.wait_ge(sm.h["s"], m_s[nb])
                    m_g.append(G2.issue(gp, lambda dc=dc, tt=tt, nb=nb:
                               gp.dma_start(
                        out=y2_r[:, tt, 512 * dc:512 * (dc + 1)],
                        in_=mtl[nb % 2][:])))
                    emit_prepass(gp, 1)
                SEC_G(_gm)
                emit_prepass_reduce()

        barrier()
        A3b.free()

        A3c1 = A(nc)
        ytl3 = [A3c1.sbuf(f"ytl3{i}", [P, DM], mybir.dt.float16)
                for i in range(2)]
        psU = [A3c1.psum(f"psU{i}", [P, 512], F32) for i in range(2)]
        psD3 = [A3c1.psum(f"psD3{i}", [P, 512], F32) for i in range(2)]
        A3c = A(nc)
        upst = [A3c.sbuf(f"upst{i}", [P, KC, 128], F32) for i in range(2)]
        upb = [A3c.sbuf(f"upb{i}", [P, KC, 128], BF16) for i in range(2)]
        h2T = A3c.sbuf("h2T", [P, IG, T], BF16)
        dwst = [A3c.sbuf(f"dwst{i}", [P, DM], F32) for i in range(2)]
        dwb = A3c.sbuf("dwb", [P, IG, DM], BF16)
        rbt = [A3c.sbuf(f"rbt{i}", [P, 512], BF16) for i in range(2)]

        dn_t_final = [sm.val("t")]
        UPL = Stream(sm, ("dA", "dB"))
        DWL = Stream(sm, ("dC", "dD"))
        uv_prev = None
        for grp in range(IG):
            u_w = []
            u_v = []
            u_t = []
            u_s = []
            bsu = sm.val("s")

            for j in range(IG):
                def _syu(sy, grp=grp, j=j, uv_prev=uv_prev):
                    i = IG * grp + j
                    if j >= 2:
                        sy.wait_ge(sm.h["v"], u_v[j - 2])
                    elif uv_prev is not None:
                        sy.wait_ge(sm.h["v"], uv_prev[IG - 2 + j])
                    u_w.append((
                        UPL.issue(sy, lambda i=i, j=j: sy.dma_start(
                            out=upst[j % 2][:],
                            in_=up_r[:, :, 128 * i:128 * (i + 1)])),
                        DWL.issue(sy, lambda i=i, j=j: sy.dma_start(
                            out=dwst[j % 2][:], in_=down_r[:, i, :]))))
                block.sync(_syu)

                def _vu(vec, grp=grp, j=j):
                    wt(vec, u_w[j][0])
                    wt(vec, u_w[j][1])
                    if j == 0:
                        vec.wait_ge(sm.h["t"], dn_t_final[0])
                    if j >= 2:
                        vec.wait_ge(sm.h["t"], u_t[4 * j - 5])
                    vec.tensor_copy(upb[j % 2][:], upst[j % 2][:])
                    sm.inc("v", vec.tensor_copy(dwb[:, j, :], dwst[j % 2][:]))
                    u_v.append(sm.val("v"))
                SEC_V(_vu)

                def _tu(te, j=j, bsu=bsu):
                    te.wait_ge(sm.h["v"], u_v[j])
                    for sl in range(4):
                        nb = 4 * j + sl
                        if nb >= 2:
                            te.wait_ge(sm.h["s"], bsu + 2 * nb - 3)
                        for kd in range(KC):
                            mm = te.matmul(psU[nb % 2][:], upb[j % 2][:, kd, :],
                                           SA[:, kd, 512 * sl:512 * (sl + 1)],
                                           start=(kd == 0), stop=(kd == KC - 1))
                        sm.inc("t", mm)
                        u_t.append(sm.val("t"))
                block.tensor(_tu)

            def _su(sca, grp=grp):
                for j in range(IG):
                    for sl in range(4):
                        nb = 4 * j + sl
                        sca.wait_ge(sm.h["t"], u_t[nb])
                        sca.activation(rbt[nb % 2][:], psU[nb % 2][:], AF.Relu)
                        u_s.append(sm.val("s"))
                        sca.activation(h2T[:, j, 512 * sl:512 * (sl + 1)],
                                       rbt[nb % 2][:], AF.Square)
            SEC_S(_su)

            h_done = sm.val("s")
            d_t = []
            d_s = []
            d_g = []
            bgg2 = G2.fin()
            g2base = G2.i
            bsd = sm.val("s")

            def _tdn(te, h_done=h_done, bsd=bsd):
                te.wait_ge(sm.h["s"], h_done)
                for tt in range(NTT):
                    for dc in range(4):
                        nb = 4 * tt + dc
                        if nb >= 2:
                            te.wait_ge(sm.h["s"], bsd + nb - 1)
                        for j in range(IG):
                            mm = te.matmul(psD3[nb % 2][:],
                                           h2T[:, j, 128 * tt:128 * (tt + 1)],
                                           dwb[:, j, 512 * dc:512 * (dc + 1)],
                                           start=(j == 0), stop=(j == IG - 1))
                        sm.inc("t", mm)
                        d_t.append(sm.val("t"))
            block.tensor(_tdn)
            dn_t_final[0] = sm.val("t")

            def _sdn(sca, bgg2=bgg2, g2base=g2base):
                for tt in range(NTT):
                    if tt >= 2:
                        wt(sca, G2.future(tt - 2 - (G2.i - g2base)))
                    else:
                        wt_all(sca, bgg2)
                    for dc in range(4):
                        nb = 4 * tt + dc
                        sca.wait_ge(sm.h["t"], d_t[nb])
                        sm.inc("s", sca.copy(
                            ytl3[tt % 2][:, 512 * dc:512 * (dc + 1)],
                            psD3[nb % 2][:]))
                        d_s.append(sm.val("s"))
            SEC_S(_sdn)

            def _gdn(gp, bgg2=bgg2):
                wt_all(gp, bgg2)
                for tt in range(NTT):
                    gp.wait_ge(sm.h["s"], d_s[4 * tt + 3])
                    d_g.append(G2.issue(gp, lambda tt=tt: gp.dma_start(
                        out=y2_r[:, tt, :], in_=ytl3[tt % 2][:],
                        accum_op=ALU.add)))
                    emit_prepass(gp, 1)
            SEC_G(_gdn)
            emit_prepass_reduce()
            uv_prev = u_v

        emit_prepass_drain()
        assert pp_state["i"] == len(pp_plan), "prepass incomplete at build"
        A3c.free()

        A3d = A(nc)
        gB = A3d.sbuf("gB", [P, DM], F32)
        bB = A3d.sbuf("bB", [P, DM], F32)
        lrow = [A3d.sbuf(f"lrow{i}", [1, 512], F32) for i in range(2)]
        x1tk = A3d.sbuf("x1tk", [P, DM], BF16)
        aptmp = A3d.sbuf("aptmp", [P, DM], F32)
        x2t = A3d.sbuf("x2t", [P, DM], BF16)
        ytl3 = [A3d.sbuf(f"y2ld{i}", [P, DM], F32) for i in range(2)]
        pstp3 = [A3d.psum(f"pstp3{i}", [P, 1024], BF16) for i in range(2)]
        psD = [A3d.psum(f"psDd{i}", [P, 512], F32) for i in range(2)]

        MS3 = Stream(sm, ("dE",), serial=True)
        w_ln = []

        gb_t = []
        gb_s = []
        for c4 in range(8):
            srcrow = ln_g if c4 < 4 else ln_b
            dstB = gB if c4 < 4 else bB
            cc = c4 % 4

            def _syg(sy, c4=c4, srcrow=srcrow, cc=cc, dnt=dn_t_final[0]):
                if c4 >= 2:
                    sy.wait_ge(sm.h["t"], gb_t[c4 - 2])
                else:
                    sy.wait_ge(sm.h["t"], dnt)
                w_ln.append(MS3.issue(sy, lambda: sy.dma_start(
                    out=lrow[c4 % 2][:],
                    in_=srcrow[None, 512 * cc:512 * (cc + 1)])))
            block.sync(_syg)

            def _tg(te, c4=c4, cc=cc):
                wt(te, w_ln[c4])
                if c4 >= 2:
                    te.wait_ge(sm.h["s"], gb_s[c4 - 2])
                sm.inc("t", te.matmul(psD[c4 % 2][:], ones_r_f[:],
                                      lrow[c4 % 2][:],
                                      start=True, stop=True))
                gb_t.append(sm.val("t"))
            block.tensor(_tg)

            def _sg(sca, c4=c4, dstB=dstB, cc=cc):
                sca.wait_ge(sm.h["t"], gb_t[c4])
                sm.inc("s", sca.copy(dstB[:, 512 * cc:512 * (cc + 1)],
                                     psD[c4 % 2][:]))
                gb_s.append(sm.val("s"))
            SEC_S(_sg)

        prev_ap3 = sm.val("s")
        y2w = []
        tp3_t = []
        tp3_s = []
        vz_l = []
        Y2L = Stream(sm, ("w", "dF"))
        for tt in range(NTT):
            def _syy(sy, tt=tt, g2y=d_g[tt], dnt=dn_t_final[0]):
                wt(sy, g2y)
                if tt >= 2:
                    sy.wait_ge(sm.h["v"], vz_l[tt - 2])
                else:
                    sy.wait_ge(sm.h["t"], dnt)
                y2w.append(Y2L.issue(sy, lambda tt=tt: sy.dma_start(
                    out=ytl3[tt % 2][:], in_=y2_r[:, tt, :])))
            block.sync(_syy)

            v_ln, s_ln = emit_ln(ytl3[tt % 2][:], DM,
                                 pre_wait=(y2w[tt],))

            def _vap3(vec, tt=tt):
                sm.inc("v", vec.tensor_scalar(aptmp[:], ytl3[tt % 2][:],
                                              st_nm[:], st_rs[:],
                                              ALU.add, ALU.mult))
            SEC_V(_vap3)
            s_ap = sm.val("s")

            def _ttp3(te, tt=tt, s_ap=s_ap):
                for half in range(2):
                    nb = 4 * tt + half
                    if len(tp3_s) >= 1 and nb >= 2:
                        wt(te, tp3_s[nb - 2])
                    mm = None
                    for j in range(8):
                        k = 8 * half + j
                        mm = te.transpose(pstp3[half][:, 128 * j:128 * (j + 1)],
                                          SA[:, k, 128 * tt:128 * (tt + 1)],
                                          ident_b[:])
                    sm.inc("t", mm)
                    tp3_t.append(sm.val("t"))
            block.tensor(_ttp3)

            def _sev3(sca, tt=tt):
                for half in range(2):
                    sca.wait_ge(sm.h["t"], tp3_t[4 * tt + half])
                    dv = x1tk[:].rearrange("p (a c) -> p a c", c=128)
                    sca.copy(dv[:, 8 * half:8 * half + 8, :],
                             pstp3[half][:])
                    tp3_s.append(("s", sm.val("s")))
            SEC_S(_sev3)
            s_x1 = ("s", sm.val("s"))

            def _vz(vec, tt=tt, s_x1=s_x1):
                wt(vec, s_x1)
                vec.tensor_tensor(ytl3[tt % 2][:], aptmp[:], gB[:], ALU.mult)
                vec.tensor_tensor(ytl3[tt % 2][:], ytl3[tt % 2][:], bB[:], ALU.add)
                sm.inc("v", vec.tensor_tensor(aptmp[:], ytl3[tt % 2][:], x1tk[:],
                                              ALU.add))
            SEC_V(_vz)
            v_z = sm.val("v")
            vz_l.append(v_z)

            v_ln2, s_ln2 = emit_ln(aptmp[:], DM, pre_wait=(("v", v_z),))

            def _vap3b(vec, tt=tt):
                if tt >= 1:
                    vec.wait_ge(sm.h["t"], tp3_t[4 * (tt - 1) + 3])
                sm.inc("v", vec.tensor_scalar(x2t[:], aptmp[:],
                                              st_nm[:], st_rs[:],
                                              ALU.add, ALU.mult))
            SEC_V(_vap3b)
            s_x2 = sm.val("v")

            def _ttp3b(te, tt=tt, s_x2=s_x2):
                te.wait_ge(sm.h["v"], s_x2)
                for half in range(2):
                    mm = None
                    for j in range(8):
                        k = 8 * half + j
                        mm = te.transpose(pstp3[half][:, 128 * j:128 * (j + 1)],
                                          x2t[:, 128 * k:128 * (k + 1)], ident_b[:])
                    sm.inc("t", mm)
                    tp3_t.append(sm.val("t"))
            block.tensor(_ttp3b)

            def _sev3b(sca, tt=tt):
                for half in range(2):
                    sca.wait_ge(sm.h["t"], tp3_t[4 * tt + 2 + half])
                    dst = SA[:, 8 * half:8 * half + 8, 128 * tt:128 * (tt + 1)]
                    sca.copy(dst, pstp3[half][:])
                    tp3_s.append(("s", sm.val("s")))
            SEC_S(_sev3b)

        barrier()
        if STOP == "x2":
            emit_dump_stream(pstp3, [x1tk, x2t], [aptmp, ytl3[0]])
            A3d.free()
            A3c1.free()
            return nc
        A3d.free()
        A3c1.free()

        A4o = A(nc)
        pstp4 = [A4o.psum(f"pstp4{i}", [P, 1024], BF16) for i in range(2)]
        A4a = A(nc)
        wst4 = [A4a.sbuf(f"wst4{i}", [P, 2, DM], F32) for i in range(2)]
        wcb4 = A4a.sbuf("wcb4", [P, 2, DM], BF16)
        WTg = A4a.sbuf("WTg", [P, KC, 128], BF16)
        WTv = A4a.sbuf("WTv", [P, KC, 128], BF16)
        bgc = A4a.sbuf("bgc", [P, 32], F32)
        gTc = [A4a.sbuf(f"gTc{i}", [P, T], F32) for i in range(2)]
        vTc = [A4a.sbuf(f"vTc{i}", [P, T], F32) for i in range(2)]
        hTc = A4a.sbuf("hTc", [P, T], F32)
        hcb = A4a.sbuf("hcb", [P, T], BF16)
        psG = [A4a.psum(f"psG{i}", [P, 512], F32) for i in range(2)]
        psV = [A4a.psum(f"psV{i}", [P, 512], F32) for i in range(2)]

        w_b1 = [None]

        def _sy4(sy):
            with nc.allow_non_contiguous_dma(reason="tiny bias"):
                w_b1[0] = Stream(sm, ("dA",)).issue(sy, lambda: sy.dma_start(
                    out=bgc[:], in_=b1_r[:]))
        block.sync(_sy4)

        c_w = []
        c_v = []
        c_tp = []
        c_se = []
        c_tm = []
        c_sev = []
        c_vs = []
        c_g = []
        bs4 = sm.val("s")
        W4L = Stream(sm, ("dB", "dC", "dD", "dE"))
        for c in range(KC):
            def _syw(sy, c=c):
                if c >= 2:
                    sy.wait_ge(sm.h["v"], c_v[c - 2])
                c_w.append((
                    W4L.issue(sy, lambda c=c: sy.dma_start(
                        out=wst4[c % 2][:, 0, :], in_=w1_r[:, c, :])),
                    W4L.issue(sy, lambda c=c: sy.dma_start(
                        out=wst4[c % 2][:, 1, :], in_=w1_r[:, KC + c, :]))))
            block.sync(_syw)

            def _vw(vec, c=c):
                wt(vec, c_w[c][0])
                wt(vec, c_w[c][1])
                if c >= 1:
                    vec.wait_ge(sm.h["t"], c_tp[4 * c - 1])
                sm.inc("v", vec.tensor_copy(wcb4[:], wst4[c % 2][:]))
                c_v.append(sm.val("v"))
            SEC_V(_vw)

            def _ttw(te, c=c, bs4=bs4):
                te.wait_ge(sm.h["v"], c_v[c])
                nb = 0
                for gv in range(2):
                    for kd in range(KC):
                        buf = nb // 8
                        j = nb % 8
                        if j == 0 and (4 * c + buf) >= 2:
                            gb = 4 * c + buf - 2
                            te.wait_ge(sm.h["s"],
                                       bs4 + 12 * (gb // 4) + (gb % 4) + 1)
                        mm = te.transpose(pstp4[buf % 2][:, 128 * j:128 * (j + 1)],
                                          wcb4[:, gv, 128 * kd:128 * (kd + 1)],
                                          ident_b[:])
                        if j == 7:
                            sm.inc("t", mm)
                            c_tp.append(sm.val("t"))
                        nb += 1
            block.tensor(_ttw)

            def _sw(sca, c=c):
                for b in range(4):
                    sca.wait_ge(sm.h["t"], c_tp[4 * c + b])
                    dstW = WTg if b < 2 else WTv
                    k0 = (b % 2) * 8
                    sm.inc("s", sca.copy(dstW[:, k0:k0 + 8, :], pstp4[b % 2][:]))
                    c_se.append(sm.val("s"))
            SEC_S(_sw)
            s_wt = sm.val("s")

            def _tmm(te, c=c, s_wt=s_wt):
                te.wait_ge(sm.h["s"], s_wt)
                for sl in range(4):
                    if sl >= 2:
                        te.wait_ge(sm.h["s"], s_wt + sl - 1)
                    for kd in range(KC):
                        mm = te.matmul(psG[sl % 2][:], WTg[:, kd, :],
                                       SA[:, kd, 512 * sl:512 * (sl + 1)],
                                       start=(kd == 0), stop=(kd == KC - 1))
                    sm.inc("t", mm)
                    c_tm.append(sm.val("t"))
                for sl in range(4):
                    if sl >= 2:
                        te.wait_ge(sm.h["s"], s_wt + 4 + sl - 1)
                    for kd in range(KC):
                        mm = te.matmul(psV[sl % 2][:], WTv[:, kd, :],
                                       SA[:, kd, 512 * sl:512 * (sl + 1)],
                                       start=(kd == 0), stop=(kd == KC - 1))
                    sm.inc("t", mm)
                    c_tm.append(sm.val("t"))
            block.tensor(_tmm)

            def _sact(sca, c=c):
                if c == 0:
                    wt(sca, w_b1[0])
                if c >= 2:
                    sca.wait_ge(sm.h["v"], c_vs[c - 2])
                base = 8 * c
                for sl in range(4):
                    sca.wait_ge(sm.h["t"], c_tm[base + sl])
                    sca.activation(gTc[c % 2][:, 512 * sl:512 * (sl + 1)],
                                   psG[sl % 2][:],
                                   AF.Sigmoid, bias=bgc[:, c:c + 1])
                    c_sev.append(sm.val("s"))
                for sl in range(4):
                    sca.wait_ge(sm.h["t"], c_tm[base + 4 + sl])
                    sca.activation(vTc[c % 2][:, 512 * sl:512 * (sl + 1)],
                                   psV[sl % 2][:],
                                   AF.Tanh, bias=bgc[:, KC + c:KC + c + 1])
                    c_sev.append(sm.val("s"))
            SEC_S(_sact)
            s_pr = sm.val("s")

            def _vscan(vec, c=c, s_pr=s_pr):
                vec.wait_ge(sm.h["s"], s_pr)
                if c >= 1:
                    wt(vec, c_g[c - 1])
                vec.tensor_tensor_scan(out=hTc[:], data0=gTc[c % 2][:],
                                       data1=vTc[c % 2][:],
                                       initial=0.0, op0=ALU.mult, op1=ALU.add)
                sm.inc("v", vec.tensor_copy(hcb[:], hTc[:]))
                c_vs.append(sm.val("v"))
            SEC_V(_vscan)

            def _gh(gp, c=c):
                gp.wait_ge(sm.h["v"], c_vs[c])
                c_g.append(G2.issue(gp, lambda c=c: gp.dma_start(
                    out=h_sp[c, :, :], in_=hcb[:])))
            SEC_G(_gh)

        barrier()
        A4a.free()

        A4b = A(nc)
        hld = [A4b.sbuf(f"hld{i}", [P, KC, 128], BF16) for i in range(4)]
        zfm = A4b.sbuf("zfm", [P, KC, 128], BF16)
        ztok = A4b.sbuf("ztok", [P, DM], BF16)
        x3f = A4b.sbuf("x3f", [P, DM], F32)
        x3b = [A4b.sbuf(f"x3b{i}", [P, DM], BF16) for i in range(2)]
        q4a = A4b.sbuf("q4a", [P, DM], F32)
        q4i = A4b.sbuf("q4i", [P, DM], I32)
        q4b = A4b.sbuf("q4b", [P, DM], F32)
        xqt = [A4b.sbuf(f"xqt{i}", [P, DM], BF16) for i in range(2)]

        z_w = []
        z_v = []
        z_t = []
        z_s = []
        z_q = []
        prev_ap4 = sm.val("s")
        HLD = Stream(sm, ("dA", "dB", "dC", "dD"))
        for tt in range(NTT):
            def _syh(sy, tt=tt, g2a=c_g[-1], g2b_=c_g[-2]):
                wt(sy, g2a)
                wt(sy, g2b_)
                if tt >= 4:
                    sy.wait_ge(sm.h["v"], z_v[tt - 4])
                z_w.append(HLD.issue(sy, lambda tt=tt: sy.dma_start(
                    out=hld[tt % 4][:],
                    in_=h_sp[:, :, 128 * tt:128 * (tt + 1)].rearrange(
                        "k p c -> p k c"))))
            block.sync(_syh)

            def _vz4(vec, tt=tt):
                wt(vec, z_w[tt])
                if tt >= 1:
                    vec.wait_ge(sm.h["t"], z_t[2 * tt - 1])
                sm.inc("v", vec.tensor_tensor(
                    zfm[:], SA[:, :, 128 * tt:128 * (tt + 1)], hld[tt % 4][:],
                    ALU.add))
                z_v.append(sm.val("v"))
            SEC_V(_vz4)
            v_zfm = sm.val("v")

            def _tz(te, tt=tt, v_zfm=v_zfm):
                te.wait_ge(sm.h["v"], v_zfm)
                for half in range(2):
                    nb = 2 * tt + half
                    if nb >= 2:
                        wt(te, z_s[nb - 2])
                    mm = None
                    for j in range(8):
                        k = 8 * half + j
                        mm = te.transpose(pstp4[half][:, 128 * j:128 * (j + 1)],
                                          zfm[:, 8 * half + j, :], ident_b[:])
                    sm.inc("t", mm)
                    z_t.append(sm.val("t"))
            block.tensor(_tz)

            def _sz(sca, tt=tt):
                for half in range(2):
                    sca.wait_ge(sm.h["t"], z_t[2 * tt + half])
                    dv = ztok[:].rearrange("p (a c) -> p a c", c=128)
                    sca.copy(dv[:, 8 * half:8 * half + 8, :],
                             pstp4[half][:])
                    z_s.append(("s", sm.val("s")))
            SEC_S(_sz)
            s_zt = ("s", sm.val("s"))

            v_ln, s_ln = emit_ln(ztok[:], DM,
                                 pre_wait=(s_zt,))

            def _vap4(vec, tt=tt):
                sm.inc("v", vec.tensor_scalar(x3f[:], ztok[:],
                                              st_nm[:], st_rs[:],
                                              ALU.add, ALU.mult))
            SEC_V(_vap4)
            prev_ap4 = sm.val("s")

            def _vq4(vec, tt=tt):
                if tt >= 2:
                    wt(vec, z_q[2 * (tt - 2)])
                    wt(vec, z_q[2 * (tt - 2) + 1])
                vec.tensor_copy(x3b[tt % 2][:], x3f[:])
                vec.tensor_reduce(qgA[:], x3f[:], axis=AXX, op=ALU.max,
                                  apply_absolute_value=True)
                sm.inc("v", vec.tensor_scalar(qgA[:], qgA[:], 1e-5, None, ALU.max))
            SEC_V(_vq4)
            v_g3 = sm.val("v")

            def _scp(sca, v_g3=v_g3):
                sca.wait_ge(sm.h["v"], v_g3)
                sm.inc("s", sca.copy(qgB[:], qgA[:]))
            SEC_S(_scp)
            s_g3 = sm.val("s")

            def _vq4b(vec, tt=tt, s_g3=s_g3):
                vec.wait_ge(sm.h["s"], s_g3)
                vec.reciprocal(scale3[:, tt:tt + 1], qgB[:])
                vec.tensor_scalar(scale3[:, tt:tt + 1], scale3[:, tt:tt + 1],
                                  127.0, None, ALU.mult)
                vec.tensor_scalar(g3r[:, tt:tt + 1], qgA[:], I127, None, ALU.mult)
                vec.tensor_scalar(q4a[:], x3f[:], scale3[:, tt:tt + 1], RD,
                                  ALU.mult, ALU.add)
                vec.tensor_copy(q4i[:], q4a[:])
                vec.tensor_scalar(q4b[:], q4i[:], -RD, None, ALU.add)
                vec.tensor_tensor(q4a[:], q4a[:], q4b[:], ALU.subtract)
                sm.inc("v", vec.scalar_tensor_tensor(
                    out=xqt[tt % 2][:], in0=q4a[:], scalar=RDT, in1=q4b[:],
                    op0=ALU.is_gt, op1=ALU.add))
            SEC_V(_vq4b)
            v_q = sm.val("v")

            def _sq4(sca, tt=tt, v_q=v_q):
                sca.wait_ge(sm.h["v"], v_q)
                z_q.append(QS.issue(sca, lambda tt=tt: sca.dma_start(
                    out=x3_r[:, tt, :], in_=x3b[tt % 2][:])))
                z_q.append(XQS.issue(sca, lambda tt=tt: sca.dma_start(
                    out=x3q_r[:, tt, :], in_=xqt[tt % 2][:])))
            SEC_S(_sq4)

        barrier()
        if STOP == "x3":
            AD = A(nc)
            d1 = [AD.sbuf(f"d1{i}", [P, DM], BF16) for i in range(2)]
            d2 = [AD.sbuf(f"d2{i}", [P, DM], F32) for i in range(2)]
            dw = []
            dv = []
            do = []
            XDL = Stream(sm, ("dC", "dD"))
            for tt in range(NTT):
                def _s1(sy, tt=tt, qx=z_q[2 * tt]):
                    wt(sy, qx)
                    if tt >= 2:
                        sy.wait_ge(sm.h["v"], dv[tt - 2])
                    dw.append(XDL.issue(sy, lambda tt=tt: sy.dma_start(
                        out=d1[tt % 2][:], in_=x3_r[:, tt, :])))
                block.sync(_s1)

                def _v1d(vec, tt=tt):
                    wt(vec, dw[tt])
                    if tt >= 2:
                        wt(vec, do[tt - 2])
                    sm.inc("v", vec.tensor_copy(d2[tt % 2][:], d1[tt % 2][:]))
                    dv.append(sm.val("v"))
                SEC_V(_v1d)

                def _s2d(sy, tt=tt):
                    sy.wait_ge(sm.h["v"], dv[tt])
                    do.append(OUT.issue(sy, lambda tt=tt: sy.dma_start(
                        out=out_r[:, tt, :], in_=d2[tt % 2][:])))
                block.sync(_s2d)

            def _sf(sy):
                wt_all(sy, OUT.fin())
            block.sync(_sf)
            barrier()
            AD.free()
            A4b.free()
            A4o.free()
            ASA.free()
            return nc

        A4b.free()
        A4o.free()
        ASA.free()

        A5 = A(nc)
        SB = A5.sbuf("SB", [P, KC, T], BF16)
        A5pre = A(nc)
        xqld = [A5pre.sbuf(f"xqld{i}", [P, DM], BF16) for i in range(2)]
        pstp5 = [A5.psum(f"pstp5{i}", [P, 1024], BF16) for i in range(2)]

        XQL = Stream(sm, ("dA", "dB"))
        q_w = []
        q_t = []
        q_s = []
        for tt in range(NTT):
            def _syq(sy, tt=tt, qz=z_q[2 * tt + 1]):
                wt(sy, qz)
                if tt >= 2:
                    sy.wait_ge(sm.h["t"], q_t[2 * (tt - 2) + 1])
                q_w.append(XQL.issue(sy, lambda tt=tt: sy.dma_start(
                    out=xqld[tt % 2][:], in_=x3q_r[:, tt, :])))
            block.sync(_syq)

            def _tq(te, tt=tt):
                wt(te, q_w[tt])
                for half in range(2):
                    nb = 2 * tt + half
                    if nb >= 2:
                        te.wait_ge(sm.h["s"], q_s[nb - 2])
                    mm = None
                    for j in range(8):
                        k = 8 * half + j
                        mm = te.transpose(pstp5[half][:, 128 * j:128 * (j + 1)],
                                          xqld[tt % 2][:, 128 * k:128 * (k + 1)],
                                          ident_b[:])
                    sm.inc("t", mm)
                    q_t.append(sm.val("t"))
            block.tensor(_tq)

            def _sq(sca, tt=tt):
                for half in range(2):
                    sca.wait_ge(sm.h["t"], q_t[2 * tt + half])
                    dst = SB[:, 8 * half:8 * half + 8, 128 * tt:128 * (tt + 1)]
                    sm.inc("s", sca.copy(dst, pstp5[half][:]))
                    q_s.append(sm.val("s"))
            SEC_S(_sq)

        barrier()
        A5pre.free()
        psF5 = A5.psum("psF5", [P, 512], F32)
        tot4 = A5.sbuf("tot4", [1, 4], F32)
        gtotB = A5.sbuf("gtotB", [P, 4], F32)

        def _tgf(te):
            sm.inc("t", te.matmul(psF5[0:1, 0:4], onescol[:], gsum[:, 0:4],
                                  start=True, stop=True))
        block.tensor(_tgf)
        t_g1 = sm.val("t")

        def _sgf(sca, t_g1=t_g1):
            sca.wait_ge(sm.h["t"], t_g1)
            sm.inc("s", sca.copy(tot4[:], psF5[0:1, 0:4]))
        SEC_S(_sgf)
        s_g1 = sm.val("s")

        def _tgf2(te, s_g1=s_g1):
            te.wait_ge(sm.h["s"], s_g1)
            sm.inc("t", te.matmul(psF5[:, 0:4], ones_r_f[0:1, 0:P],
                                  tot4[:], start=True, stop=True))
        block.tensor(_tgf2)
        t_g2 = sm.val("t")

        def _sgf2(sca, t_g2=t_g2):
            sca.wait_ge(sm.h["t"], t_g2)
            sm.inc("s", sca.copy(gtotB[:], psF5[:, 0:4]))
        SEC_S(_sgf2)
        s_g2 = sm.val("s")

        def _vgf(vec, s_g2=s_g2):
            vec.wait_ge(sm.h["s"], s_g2)
            vec.tensor_scalar(gamf[:], gtotB[:], I2_24, None, ALU.mult)
            vec.tensor_scalar(gamh[:], gamf[:], 0.5, None, ALU.mult)
            vec.tensor_scalar(gamhn[:], gamh[:], -1.0, None, ALU.mult)
            vec.tensor_scalar(rv_g[:], g3r[:], gamf[:, 0:1], None, ALU.mult)
            sm.inc("v", vec.tensor_scalar(rv_v[:], g3r[:], gamf[:, 1:2], None,
                                          ALU.mult))
        SEC_V(_vgf)
        v_gam = sm.val("v")

        gst = A5.sbuf("gst", [P, DM], F32)
        vst = A5.sbuf("vst", [P, DM], F32)
        tb1 = A5.sbuf("tb1", [P, DM // 2], BF16)
        tb2 = tb1
        gter = [A5.sbuf(f"gter{i}", [P, DM], BF16) for i in range(3)]
        vter = [A5.sbuf(f"vter{i}", [P, DM], BF16) for i in range(3)]
        gwT = A5.sbuf("gwT", [P, KC, 512], BF16)
        vwT = A5.sbuf("vwT", [P, KC, 512], BF16)
        b5row = [A5.sbuf(f"b5row{i}", [1, 1024], F32) for i in range(2)]
        gbr5 = [b5row[i][:, 0:512] for i in range(2)]
        vbr5 = [b5row[i][:, 512:1024] for i in range(2)]
        bgB5 = [A5.sbuf(f"bgB5{i}", [P, 512], F32) for i in range(2)]
        bvB5 = [A5.sbuf(f"bvB5{i}", [P, 512], F32) for i in range(2)]
        gpre = [A5.sbuf(f"gpre{i}", [P, 512], BF16) for i in range(2)]
        gss = [A5.sbuf(f"gss{i}", [P, 512], BF16) for i in range(2)]
        vpre = [A5.sbuf(f"vpre{i}", [P, 512], BF16) for i in range(2)]
        gvsl = [A5.sbuf(f"gvsl{i}", [P, NTT, 512], BF16) for i in range(2)]
        psG5 = [A5.psum(f"psG5{i}", [P, 512], F32) for i in range(2)]
        psV5 = [A5.psum(f"psV5{i}", [P, 512], F32) for i in range(2)]
        psB5 = A5.psum("psB5", [P, 512], F32)

        bw5 = sm.val("w")

        GVW = Stream(sm, ("dC", "dD", "dE", "dF"))
        B5L = Stream(sm, ("dA", "dB"))
        sp_q = []
        t_bv_h = {}
        s_bv_h = {}
        v_used = {}
        PR = {}
        MM = {}

        def emit_bias5(hc):
            w_b5 = []

            def _syb5(sy, hc=hc):
                if hc >= 2:
                    sy.wait_ge(sm.h["t"], t_bv_h[hc - 2])
                w_b5.append(B5L.issue(sy, lambda hc=hc: sy.dma_start(
                    out=gbr5[hc % 2],
                    in_=gb1[None, 512 * hc:512 * (hc + 1)])))
                w_b5.append(B5L.issue(sy, lambda hc=hc: sy.dma_start(
                    out=vbr5[hc % 2],
                    in_=vb1[None, 512 * hc:512 * (hc + 1)])))
            block.sync(_syb5)

            def _tb5(te, hc=hc, w_b5=w_b5):
                if hc >= 1:
                    te.wait_ge(sm.h["s"], s_bv_h[hc - 1])
                wt(te, w_b5[0])
                wt(te, w_b5[1])
                sm.inc("t", te.matmul(psB5[:], ones_r_f[:], gbr5[hc % 2],
                                      start=True, stop=True))
            block.tensor(_tb5)
            t_bg = sm.val("t")

            def _sb5(sca, hc=hc, t_bg=t_bg):
                sca.wait_ge(sm.h["t"], t_bg)
                if hc >= 2:
                    sca.wait_ge(sm.h["v"], v_used[hc - 2])
                sm.inc("s", sca.copy(bgB5[hc % 2][:], psB5[:]))
            SEC_S(_sb5)
            s_bg = sm.val("s")

            def _tb5b(te, hc=hc, s_bg=s_bg):
                te.wait_ge(sm.h["s"], s_bg)
                sm.inc("t", te.matmul(psB5[:], ones_r_f[:], vbr5[hc % 2],
                                      start=True, stop=True))
            block.tensor(_tb5b)
            t_bv_h[hc] = sm.val("t")

            def _sb5b(sca, hc=hc):
                sca.wait_ge(sm.h["t"], t_bv_h[hc])
                sm.inc("s", sca.copy(bvB5[hc % 2][:], psB5[:]))
            SEC_S(_sb5b)
            s_bv_h[hc] = sm.val("s")

        def emit_load5(hc, r):
            st = PR.setdefault(hc, dict(w=[], v=[], tp=[], se=[]))

            def _syw5(sy, hc=hc, r=r, st=st):
                if r >= 1:
                    sy.wait_ge(sm.h["v"], st["v"][2 * (r - 1) + 1])
                elif hc > 0:
                    sy.wait_ge(sm.h["v"], PR[hc - 1]["v"][7])
                st["w"].append((
                    GVW.issue(sy, lambda hc=hc, r=r: sy.dma_start(
                        out=gst[:], in_=gw1_r[:, 4 * hc + r, :])),
                    GVW.issue(sy, lambda hc=hc, r=r: sy.dma_start(
                        out=vst[:], in_=vw1_r[:, 4 * hc + r, :]))))
            block.sync(_syw5)

        def emit_tern5(hc, r):
            st = PR[hc]

            def _vt5(vec, hc=hc, r=r, st=st):
                gs = 4 * hc + r
                bi = gs % 3
                if hc == 0 and r == 0:
                    vec.wait_ge(sm.h["v"], v_gam)
                wt(vec, st["w"][r][0])
                wt(vec, st["w"][r][1])
                if gs >= 3:
                    hp, rp = (gs - 3) // 4, (gs - 3) % 4
                    vec.wait_ge(sm.h["t"], PR[hp]["tp"][4 * rp + 3])
                H2 = DM // 2
                for hf in range(2):
                    vec.tensor_scalar(tb1[:], gst[:, hf * H2:(hf + 1) * H2],
                                      gamhn[:, 0:1], None, ALU.is_lt)
                    vec.scalar_tensor_tensor(
                        out=gter[bi][:, hf * H2:(hf + 1) * H2],
                        in0=gst[:, hf * H2:(hf + 1) * H2], scalar=gamh[:, 0:1],
                        in1=tb1[:], op0=ALU.is_gt, op1=ALU.subtract)
                st["v"].append(sm.val("v"))
                for hf in range(2):
                    vec.tensor_scalar(tb2[:], vst[:, hf * H2:(hf + 1) * H2],
                                      gamhn[:, 1:2], None, ALU.is_lt)
                    vec.scalar_tensor_tensor(
                        out=vter[bi][:, hf * H2:(hf + 1) * H2],
                        in0=vst[:, hf * H2:(hf + 1) * H2], scalar=gamh[:, 1:2],
                        in1=tb2[:], op0=ALU.is_gt, op1=ALU.subtract)
                st["v"].append(sm.val("v"))
            SEC_V(_vt5)

        def emit_tp5(hc, r):
            st = PR[hc]
            bi = (4 * hc + r) % 3
            for si in range(4):
                b = 4 * r + si
                src = gter[bi] if si < 2 else vter[bi]
                half = si % 2

                def _tt5(te, hc=hc, r=r, st=st, si=si, b=b, src=src,
                         half=half):
                    if si == 0:
                        te.wait_ge(sm.h["v"], st["v"][2 * r])
                    elif si == 2:
                        te.wait_ge(sm.h["v"], st["v"][2 * r + 1])
                    if b >= 2:
                        te.wait_ge(sm.h["s"], st["se"][b - 2])
                    elif hc > 0:
                        te.wait_ge(sm.h["s"], PR[hc - 1]["se"][14 + b])
                    mm = None
                    for j in range(8):
                        kd = 8 * half + j
                        mm = te.transpose(
                            pstp5[b % 2][:, 128 * j:128 * (j + 1)],
                            src[:, 128 * kd:128 * (kd + 1)], ident_b[:])
                    sm.inc("t", mm)
                    st["tp"].append(sm.val("t"))
                block.tensor(_tt5)

                def _st5(sca, hc=hc, r=r, st=st, si=si, b=b, half=half):
                    sca.wait_ge(sm.h["t"], st["tp"][b])
                    if hc > 0 and b == 0:
                        sca.wait_ge(sm.h["t"], MM[hc - 1][-1])
                    dstW = gwT if si < 2 else vwT
                    dv = dstW[:, 8 * half:8 * half + 8,
                              128 * r:128 * (r + 1)]
                    sm.inc("s", sca.copy(dv, pstp5[b % 2][:]))
                    st["se"].append(sm.val("s"))
                SEC_S(_st5)

        for hc in range(16):
            emit_bias5(hc)
            if hc == 0:
                for r in range(4):
                    emit_load5(0, r)
                    emit_tern5(0, r)
                    emit_tp5(0, r)
            else:
                emit_tp5(hc, 0)
                emit_tern5(hc, 3)
                for r in range(1, 4):
                    emit_tp5(hc, r)
            wt_done = sm.val("s")
            s_bv = s_bv_h[hc]

            mm_t = []
            MM[hc] = mm_t
            ev_v = []
            si_s = []
            gv_v = []
            for tt in range(NTT):
                def _tm5(te, hc=hc, tt=tt, s_bv=s_bv, wt_done=wt_done):
                    if tt == 0:
                        te.wait_ge(sm.h["s"], max(s_bv, wt_done))
                    if tt >= 2:
                        te.wait_ge(sm.h["v"], ev_v[2 * (tt - 2) + 1])
                    for kd in range(KC):
                        mm = te.matmul(psG5[tt % 2][:],
                                       SB[:, kd, 128 * tt:128 * (tt + 1)],
                                       gwT[:, kd, :],
                                       start=(kd == 0), stop=(kd == KC - 1))
                    sm.inc("t", mm)
                    mm_t.append(sm.val("t"))
                    for kd in range(KC):
                        mm = te.matmul(psV5[tt % 2][:],
                                       SB[:, kd, 128 * tt:128 * (tt + 1)],
                                       vwT[:, kd, :],
                                       start=(kd == 0), stop=(kd == KC - 1))
                    sm.inc("t", mm)
                    mm_t.append(sm.val("t"))
                block.tensor(_tm5)

                def _ve5(vec, hc=hc, tt=tt):
                    vec.wait_ge(sm.h["t"], mm_t[2 * tt])
                    if tt >= 2:
                        vec.wait_ge(sm.h["s"], si_s[tt - 2])
                    sm.inc("v", vec.scalar_tensor_tensor(
                        out=gpre[tt % 2][:], in0=psG5[tt % 2][:],
                        scalar=rv_g[:, tt:tt + 1],
                        in1=bgB5[hc % 2][:], op0=ALU.mult, op1=ALU.add))
                    ev_v.append(sm.val("v"))
                    vec.wait_ge(sm.h["t"], mm_t[2 * tt + 1])
                    if tt >= 2:
                        vec.wait_ge(sm.h["v"], gv_v[tt - 2])
                    sm.inc("v", vec.scalar_tensor_tensor(
                        out=vpre[tt % 2][:], in0=psV5[tt % 2][:],
                        scalar=rv_v[:, tt:tt + 1],
                        in1=bvB5[hc % 2][:], op0=ALU.mult, op1=ALU.add))
                    ev_v.append(sm.val("v"))
                SEC_V(_ve5)

                def _ss5(sca, tt=tt):
                    sca.wait_ge(sm.h["v"], ev_v[2 * tt])
                    sm.inc("s", sca.activation(gss[tt % 2][:], gpre[tt % 2][:],
                                               AF.Silu))
                    si_s.append(sm.val("s"))
                SEC_S(_ss5)

                def _vg5(vec, hc=hc, tt=tt):
                    vec.wait_ge(sm.h["s"], si_s[tt])
                    if tt == 0 and hc >= 2:
                        wt(vec, sp_q[hc - 2])
                    vec.tensor_tensor(gvsl[hc % 2][:, tt, :], gss[tt % 2][:],
                                      vpre[tt % 2][:], ALU.mult)
                    vec.tensor_reduce(qgA[:], gvsl[hc % 2][:, tt, :], axis=AXX,
                                      op=ALU.max, apply_absolute_value=True)
                    sm.inc("v2", vec.tensor_tensor(
                        ggv[:, tt:tt + 1], ggv[:, tt:tt + 1], qgA[:], ALU.max))
                    gv_v.append(sm.val("v"))
                SEC_V(_vg5)

                if hc < 15 and tt % 4 == 0:
                    emit_load5(hc + 1, tt // 4)
                if hc < 15 and tt % 4 == 2 and tt // 4 < 3:
                    emit_tern5(hc + 1, tt // 4)

            gv_done = sm.val("v")

            def _sq5(sca, hc=hc, gv_done=gv_done):
                sca.wait_ge(sm.h["v"], gv_done)
                sp_q.append(QS.issue(sca, lambda hc=hc: sca.dma_start(
                    out=gv_sp[hc].rearrange("tt p c -> p tt c"),
                    in_=gvsl[hc % 2][:])))
            SEC_S(_sq5)
            v_used[hc] = sm.val("v")

        barrier()
        A5.free()

        A6 = A(nc)
        E8 = 8
        KE = 8
        gvqT = [A6.sbuf(f"gvqT{i}", [P, KE, T], BF16) for i in range(2)]
        owT6 = [A6.sbuf(f"owT{i}", [P, KE, T], BF16) for i in range(2)]
        rcp6 = A6.sbuf("rcp6", [P, NTT], F32)
        otile = [A6.sbuf(f"otile{i}", [P, DM], mybir.dt.float16) for i in range(2)]
        psF6 = [A6.psum(f"psF6{i}", [P, 512], F32) for i in range(4)]
        A6b = A(nc)
        gvld = A6b.sbuf("gvld0", [P, NTT, 512], BF16)
        owst = A6b.sbuf("owst", [P, KC, 256], F32)
        owtb = A6b.sbuf("owtb", [P, KC, 256], BF16)
        tb6 = A6b.sbuf("tb6", [P, KC // 2, 256], BF16)
        q6a = lnscr[:, 0:512]
        q6b = lnscr[:, 512:1024]
        q6i = A6b.sbuf("q6i", [P, 512], I32)
        xq6 = [A6b.sbuf(f"xq6{i}", [P, 512], BF16) for i in range(2)]
        pstp6 = [A6b.psum(f"pstp6{i}", [P, 512], BF16) for i in range(2)]
        pstp6o = [A6b.psum(f"pstp6o{i}", [P, 1024], BF16) for i in range(2)]

        def _v6p(vec):
            sm.inc("v", vec.tensor_scalar(rcp6[:], ggv[:], 1e-5, None, ALU.max))
        SEC_V(_v6p)
        v_p6 = sm.val("v")

        def _s6p(sca, v_p6=v_p6):
            sca.wait_ge(sm.h["v"], v_p6)
            sm.inc("s", sca.copy(scgv[:], rcp6[:]))
        SEC_S(_s6p)
        s_p6 = sm.val("s")

        def _v6p2(vec, s_p6=s_p6):
            vec.wait_ge(sm.h["s"], s_p6)
            vec.reciprocal(gvr[:], scgv[:])
            vec.tensor_scalar(scgv[:], gvr[:], 127.0, None, ALU.mult)
            vec.tensor_scalar(gvr[:], rcp6[:], I127, None, ALU.mult)
            sm.inc("v", vec.tensor_scalar(rv_o[:], gvr[:], gamf[:, 2:3], None,
                                          ALU.mult))
        SEC_V(_v6p2)
        v_rv6 = sm.val("v")

        GVL = Stream(sm, ("dA", "dB"))
        OWL = Stream(sm, ("dC",))
        ES = {}
        FM = {}
        QR = {}
        BQ = {0: G2.fin()}
        f_t = []
        f_s = []
        f_g = []

        def emit_gvload6(e, l):
            st = ES.setdefault(e, dict(lw=[], lv=[], lt=[], ls=[],
                                       ow=[], ov=[], ot=[], os=[]))

            def _syl(sy, e=e, l=l, st=st):
                hc = 2 * e + l
                wt(sy, sp_q[hc])
                if l >= 1:
                    sy.wait_ge(sm.h["v"], st["lv"][NTT - 1])
                elif e > 0:
                    sy.wait_ge(sm.h["v"], ES[e - 1]["lv"][2 * NTT - 1])
                st["lw"].append(GVL.issue(sy, lambda hc=hc: sy.dma_start(
                    out=gvld[:],
                    in_=gv_sp[hc].rearrange("tt p c -> p tt c"))))
            block.sync(_syl)

        def emit_gvq6(e, idx):
            st = ES[e]
            l, tt = idx // NTT, idx % NTT

            def _vq6(vec, e=e, l=l, tt=tt, idx=idx, st=st):
                if idx == 0 and e == 0:
                    vec.wait_ge(sm.h["v"], v_rv6)
                if tt == 0:
                    wt(vec, st["lw"][l])
                if idx >= 2:
                    vec.wait_ge(sm.h["t"], st["lt"][idx - 2])
                elif e > 0:
                    vec.wait_ge(sm.h["t"], ES[e - 1]["lt"][2 * NTT - 2 + idx])
                vec.tensor_scalar(q6a, gvld[:, tt, :],
                                  scgv[:, tt:tt + 1], RD,
                                  ALU.mult, ALU.add)
                vec.tensor_copy(q6i[:], q6a)
                vec.tensor_scalar(q6b, q6i[:], -RD, None, ALU.add)
                vec.tensor_tensor(q6a, q6a, q6b, ALU.subtract)
                sm.inc("v", vec.scalar_tensor_tensor(
                    out=xq6[idx % 2][:], in0=q6a, scalar=RDT, in1=q6b,
                    op0=ALU.is_gt, op1=ALU.add))
                st["lv"].append(sm.val("v"))
            SEC_V(_vq6)

        def emit_gvtp6(e, idx):
            st = ES[e]
            l, tt = idx // NTT, idx % NTT

            def _tq6(te, e=e, idx=idx, st=st):
                te.wait_ge(sm.h["v"], st["lv"][idx])
                if idx >= 2:
                    te.wait_ge(sm.h["s"], st["ls"][idx - 2])
                elif e > 0:
                    te.wait_ge(sm.h["s"], ES[e - 1]["ls"][2 * NTT - 2 + idx])
                mm = None
                for j in range(4):
                    mm = te.transpose(
                        pstp6[idx % 2][:, 128 * j:128 * (j + 1)],
                        xq6[idx % 2][:, 128 * j:128 * (j + 1)], ident_b[:])
                sm.inc("t", mm)
                st["lt"].append(sm.val("t"))
            block.tensor(_tq6)

            def _sq6(sca, e=e, l=l, tt=tt, idx=idx, st=st):
                sca.wait_ge(sm.h["t"], st["lt"][idx])
                if idx == 0 and e >= 2:
                    sca.wait_ge(sm.h["t"], FM[e - 2][-1])
                dv = gvqT[e % 2][:, 4 * l:4 * l + 4, 128 * tt:128 * (tt + 1)]
                sm.inc("s", sca.copy(dv, pstp6[idx % 2][:]))
                st["ls"].append(sm.val("s"))
            SEC_S(_sq6)

        def emit_owload6(e, m):
            st = ES[e]

            def _syo(sy, e=e, m=m, st=st):
                if m == 0 and e > 0:
                    sy.wait_ge(sm.h["v"], ES[e - 1]["ov"][3])
                if m >= 1:
                    sy.wait_ge(sm.h["v"], st["ov"][m - 1])
                st["ow"].append(OWL.issue(sy, lambda e=e, m=m: sy.dma_start(
                    out=owst[:],
                    in_=ow1_r[:, :, 1024 * e + 256 * m:
                              1024 * e + 256 * (m + 1)])))
            block.sync(_syo)

        def emit_owtern6(e, m):
            st = ES[e]

            def _vo6(vec, e=e, m=m, st=st):
                wt(vec, st["ow"][m])
                if m == 0 and e > 0:
                    vec.wait_ge(sm.h["t"], ES[e - 1]["ot"][15])
                if m >= 1:
                    vec.wait_ge(sm.h["t"], st["ot"][4 * m - 1])
                for hf in range(2):
                    hs = slice(8 * hf, 8 * (hf + 1))
                    vec.tensor_scalar(tb6[:], owst[:, hs, :],
                                      gamhn[:, 2:3], None, ALU.is_lt)
                    vec.scalar_tensor_tensor(
                        out=owtb[:, hs, :], in0=owst[:, hs, :],
                        scalar=gamh[:, 2:3],
                        in1=tb6[:], op0=ALU.is_gt, op1=ALU.subtract)
                st["ov"].append(sm.val("v"))
            SEC_V(_vo6)

        def emit_owtp6(e, m):
            st = ES[e]
            for b in range(4):
                gb_i = 4 * m + b

                def _to6(te, e=e, m=m, st=st, b=b, gb_i=gb_i):
                    if b == 0:
                        te.wait_ge(sm.h["v"], st["ov"][m])
                    if gb_i >= 2:
                        te.wait_ge(sm.h["s"], st["os"][gb_i - 2])
                    elif e > 0:
                        te.wait_ge(sm.h["s"], ES[e - 1]["os"][14 + gb_i])
                    mm = None
                    for i8 in range(8):
                        kd = 4 * b + i8 // 2
                        j = i8 % 2
                        mm = te.transpose(
                            pstp6o[gb_i % 2][:, 128 * i8:128 * (i8 + 1)],
                            owtb[:, kd, 128 * j:128 * (j + 1)], ident_b[:])
                    sm.inc("t", mm)
                    st["ot"].append(sm.val("t"))
                block.tensor(_to6)

                def _so6(sca, e=e, m=m, st=st, b=b, gb_i=gb_i):
                    sca.wait_ge(sm.h["t"], st["ot"][gb_i])
                    if gb_i == 0 and e >= 2:
                        sca.wait_ge(sm.h["t"], FM[e - 2][-1])
                    dv = owT6[e % 2][:, 2 * m:2 * m + 2,
                                     512 * b:512 * (b + 1)].rearrange(
                        "p j (k c) -> p k j c", c=128)
                    sm.inc("s", sca.copy(dv, pstp6o[gb_i % 2][:]))
                    st["os"].append(sm.val("s"))
                SEC_S(_so6)

        def emit_build6(e, tt):
            if tt == 0:
                emit_gvload6(e, 0)
            if tt == 8:
                emit_gvload6(e, 1)
            if tt >= 1:
                for idx in (2 * (tt - 1), 2 * (tt - 1) + 1):
                    emit_gvtp6(e, idx)
            for idx in (2 * tt, 2 * tt + 1):
                emit_gvq6(e, idx)
            if tt == 15:
                for idx in (30, 31):
                    emit_gvtp6(e, idx)
            if tt % 4 == 0:
                emit_owload6(e, tt // 4)
            if tt % 4 == 1:
                emit_owtern6(e, tt // 4)
            if tt % 4 == 2:
                emit_owtp6(e, tt // 4)

        emit_gvload6(0, 0)
        for tt in range(NTT):
            emit_gvq6(0, tt)
            emit_gvtp6(0, tt)
        emit_gvload6(0, 1)
        for tt in range(NTT):
            emit_gvq6(0, NTT + tt)
            emit_gvtp6(0, NTT + tt)
        for m in range(4):
            emit_owload6(0, m)
            emit_owtern6(0, m)
            emit_owtp6(0, m)
        QR[0] = sm.val("s")

        for e in range(E8):
            mm_l = []
            FM[e] = mm_l
            for tt in range(NTT):
                gtt = NTT * e + tt

                def _tf(te, e=e, tt=tt, gtt=gtt):
                    if tt == 0:
                        te.wait_ge(sm.h["s"], QR[e])
                    for dc in range(4):
                        gnb = 4 * gtt + dc
                        if gnb >= 4:
                            te.wait_ge(sm.h["s"], f_s[gnb - 4])
                        for kd in range(KE):
                            mm = te.matmul(
                                psF6[gnb % 4][:],
                                gvqT[e % 2][:, kd, 128 * tt:128 * (tt + 1)],
                                owT6[e % 2][:, kd, 512 * dc:512 * (dc + 1)],
                                start=(kd == 0), stop=(kd == KE - 1))
                        sm.inc("t", mm)
                        f_t.append(sm.val("t"))
                        mm_l.append(sm.val("t"))
                block.tensor(_tf)

                def _sf(sca, e=e, tt=tt, gtt=gtt):
                    if gtt >= 2:
                        wt(sca, f_g[gtt - 2])
                    for dc in range(4):
                        gnb = 4 * gtt + dc
                        sca.wait_ge(sm.h["t"], f_t[gnb])
                        sm.inc("s", sca.activation(
                            otile[tt % 2][:, 512 * dc:512 * (dc + 1)],
                            psF6[gnb % 4][:], AF.Copy, scale=rv_o[:, tt:tt + 1]))
                        f_s.append(sm.val("s"))
                SEC_S(_sf)

                def _gf(gp, e=e, tt=tt, gtt=gtt, bq6=BQ[e]):
                    if tt == 0:
                        wt_all(gp, bq6)
                    gp.wait_ge(sm.h["s"], f_s[4 * gtt + 3])
                    if e == 0:
                        f_g.append(G2.issue(gp, lambda tt=tt: gp.dma_start(
                            out=oacc_r[:, tt, :], in_=otile[tt % 2][:])))
                    else:
                        f_g.append(G2.issue(gp, lambda tt=tt: gp.dma_start(
                            out=oacc_r[:, tt, :], in_=otile[tt % 2][:],
                            accum_op=ALU.add)))
                SEC_G(_gf)

                if e < E8 - 1:
                    emit_build6(e + 1, tt)
            QR[e + 1] = sm.val("s")
            BQ[e + 1] = G2.fin()

        b7 = ES[E8 - 1]
        b7t = max(b7["lt"][-1], b7["ot"][-1])
        b7v = max(b7["lv"][-1], b7["ov"][-1])
        b7s = max(b7["ls"][-1], b7["os"][-1])
        A6b.free()

        A7 = A(nc)
        obB = A7.sbuf("obB", [P, DM], F32)
        obr = A7.sbuf("obr", [1, DM], F32)
        otl7 = [A7.sbuf(f"otl7{i}", [P, DM], F32) for i in range(2)]
        x3l = [A7.sbuf(f"x3l{i}", [P, DM], BF16) for i in range(2)]
        x4t = [A7.sbuf(f"x4t{i}", [P, DM], F32) for i in range(2)]
        psF7 = [A7.psum(f"psF7{i}", [P, 512], F32) for i in range(2)]

        w_ob = [None]

        def _sy7(sy):
            sy.wait_ge(sm.h["t"], b7t)
            sy.wait_ge(sm.h["v"], b7v)
            sy.wait_ge(sm.h["s"], b7s)
            w_ob[0] = Stream(sm, ("dD",)).issue(sy, lambda: sy.dma_start(
                out=obr[:], in_=ob1[None, :]))
        block.sync(_sy7)

        ob_t = []
        ob_s = []
        for c4 in range(4):
            def _t7(te, c4=c4):
                if c4 == 0:
                    wt(te, w_ob[0])
                if c4 >= 2:
                    te.wait_ge(sm.h["s"], ob_s[c4 - 2])
                sm.inc("t", te.matmul(psF7[c4 % 2][:], ones_r_f[:],
                                      obr[0:1, 512 * c4:512 * (c4 + 1)],
                                      start=True, stop=True))
                ob_t.append(sm.val("t"))
            block.tensor(_t7)

            def _s7(sca, c4=c4):
                sca.wait_ge(sm.h["t"], ob_t[c4])
                sm.inc("s", sca.copy(obB[:, 512 * c4:512 * (c4 + 1)],
                                     psF7[c4 % 2][:]))
                ob_s.append(sm.val("s"))
            SEC_S(_s7)

        s_ob = sm.val("s")
        w7 = []
        v7 = []
        o7 = []
        prev_ap7 = sm.val("s")
        O7L = Stream(sm, ("dE", "dF"))
        X7L = Stream(sm, ("w", "g3"))
        for tt in range(NTT):
            def _syl7(sy, tt=tt, g2o=f_g[7 * NTT + tt], qx=z_q[2 * tt]):
                if tt < 2:
                    sy.wait_ge(sm.h["t"], b7t)
                    sy.wait_ge(sm.h["v"], b7v)
                    sy.wait_ge(sm.h["s"], b7s)
                wt(sy, g2o)
                wt(sy, qx)
                if tt >= 2:
                    wt(sy, o7[tt - 2])
                w7.append((
                    O7L.issue(sy, lambda tt=tt: sy.dma_start(
                        out=otl7[tt % 2][:], in_=oacc_r[:, tt, :])),
                    X7L.issue(sy, lambda tt=tt: sy.dma_start(
                        out=x3l[tt % 2][:], in_=x3_r[:, tt, :]))))
            block.sync(_syl7)

            def _v7(vec, tt=tt, s_ob=s_ob):
                wt(vec, w7[tt][0])
                wt(vec, w7[tt][1])
                if tt == 0:
                    vec.wait_ge(sm.h["s"], s_ob)
                vec.tensor_tensor(otl7[tt % 2][:], otl7[tt % 2][:], obB[:], ALU.add)
                sm.inc("v", vec.tensor_tensor(otl7[tt % 2][:], otl7[tt % 2][:],
                                              x3l[tt % 2][:], ALU.add))
                v7.append(sm.val("v"))
            SEC_V(_v7)

            v_ln, s_ln = emit_ln(otl7[tt % 2][:], DM,
                                 pre_wait=(("v", v7[tt]),))

            def _vap7(vec, tt=tt):
                if tt >= 2:
                    wt(vec, o7[tt - 2])
                sm.inc("v", vec.tensor_scalar(x4t[tt % 2][:], otl7[tt % 2][:],
                                              st_nm[:], st_rs[:],
                                              ALU.add, ALU.mult))
            SEC_V(_vap7)
            s_x4 = sm.val("v")
            prev_ap7 = sm.val("s")

            def _syo7(sy, tt=tt, s_x4=s_x4):
                sy.wait_ge(sm.h["v"], s_x4)
                o7.append(OUT.issue(sy, lambda tt=tt: sy.dma_start(
                    out=out_r[:, tt, :], in_=x4t[tt % 2][:])))
            block.sync(_syo7)

        def _syf7(sy):
            wt_all(sy, OUT.fin())
        block.sync(_syf7)
        barrier()
        A7.free()
        A6.free()
        PA.free()

    return nc


def _make_consts():
    d = 128
    inv = 10000.0 ** (-(np.arange(0, d, 2, dtype=np.float64)) / d)
    ang = np.arange(T, dtype=np.float64)[:, None] * inv[None, :]
    cos = np.cos(ang).astype(np.float32)
    sin = np.sin(ang).astype(np.float32)
    trig = np.concatenate([cos, sin], axis=1)
    mk = np.zeros((P, 4, 512), np.float32)
    for j in range(4):
        s_idx = np.arange(P)[:, None] + 128 * j
        q_idx = np.arange(512)[None, :]
        mk[:, j, :] = (s_idx <= q_idx).astype(np.float32)
    return np.ascontiguousarray(trig), np.ascontiguousarray(
        mk.astype(ml_dtypes.bfloat16))


_NC_CACHE = {}


def kernel(**inputs):
    x = np.ascontiguousarray(np.asarray(inputs["L0_attn_x"], np.float32))
    assert x.shape == (NCORES, T, DM)
    trig, mk = _make_consts()

    key = STOP
    if key not in _NC_CACHE:
        _NC_CACHE[key] = build_program()
    nc = _NC_CACHE[key]

    names = ["L0_attn_Wkqv", "L0_attn_bkqv", "L0_attn_Wo", "L0_attn_bo",
             "L0_ffn_gate_w", "L0_ffn_up_w", "L0_ffn_down_w",
             "L0_ffn_expert_weight", "L0_ffn_ln_gamma", "L0_ffn_ln_beta",
             "L1_attn_W", "L1_attn_b", "L1_ffn_gate_w", "L1_ffn_gate_b",
             "L1_ffn_val_w", "L1_ffn_val_b", "L1_ffn_out_w", "L1_ffn_out_b"]
    shared = {n: np.ascontiguousarray(np.asarray(inputs[n], np.float32))
              for n in names}
    shared["c_trig2"] = trig
    shared["c_mask"] = mk

    in_maps = []
    for b in range(NCORES):
        m = dict(shared)
        m["x"] = np.ascontiguousarray(x[b])
        in_maps.append(m)

    res = run_bass_kernel_spmd(nc, in_maps, core_ids=list(range(NCORES)))
    return np.stack([np.asarray(r["out"], np.float32) for r in res.results],
                    axis=0)
